# version 01
# baseline (speedup 1.0000x reference)
import numpy as np
from contextlib import ExitStack

import concourse.bass as bass
import concourse.mybir as mybir
from concourse.bass_utils import run_bass_kernel_spmd
from concourse.bass import IndirectOffsetOnAxis

F32 = mybir.dt.float32
BF16 = mybir.dt.bfloat16
I32 = mybir.dt.int32
AF = mybir.ActivationFunctionType
ALU = mybir.AluOpType
AX = mybir.AxisListType

NCORES = 8
SEQ = 2048
DM = 1024
NFC = DM // 128
NB = 2
TOK = NB * SEQ
NTT = TOK // 128
NCH = SEQ // 128
NH, HD = 16, 64



class Sem:
    __slots__ = ("h", "n")

    def __init__(self, handle):
        self.h = handle
        self.n = 0


class PB:
    ENGINES = ("sync", "vector", "scalar", "tensor", "gpsimd")

    def __init__(self, nc, ctx):
        self.nc = nc
        self.ctx = ctx
        self.q = {e: [] for e in self.ENGINES}

    def sem(self, name):
        return Sem(self.ctx.enter_context(self.nc.semaphore(name)))

    def sb(self, name, shape, dtype):
        return self.ctx.enter_context(self.nc.sbuf_tensor(name, shape, dtype))

    def ps(self, name, shape, dtype=F32):
        return self.ctx.enter_context(self.nc.psum_tensor(name, shape, dtype))

    def emit(self, eng, fn, wait=(), inc=None):
        wl = tuple((s.h, int(v)) for s, v in wait if v and v > 0)
        tgt = None
        ret = None
        if inc is not None:
            s, k = inc
            s.n += k
            ret = s.n
            tgt = (s.h, k)

        def run(e, fn=fn, wl=wl, tgt=tgt):
            for h, v in wl:
                e.wait_ge(h, v)
            ins = fn(e)
            if tgt is not None:
                ins.then_inc(*tgt)
        self.q[eng].append(run)
        return ret

    def raw(self, eng, fn):
        self.q[eng].append(fn)

    def mm_group(self, psum_ap, pairs, wait=(), inc=None, first_start=True,
                 last_stop=True, skip_group_check=False):
        n = len(pairs)
        r = None
        for i, (lh, rh) in enumerate(pairs):
            r = self.emit(
                "tensor",
                (lambda e, lh=lh, rh=rh, st_=(first_start and i == 0),
                 sp_=(last_stop and i == n - 1), ps=psum_ap, sg=skip_group_check:
                 e.matmul(ps, lh, rh, start=st_, stop=sp_, skip_group_check=sg)),
                wait=wait if i == 0 else (),
                inc=inc if i == n - 1 else None)
        return r

    def finish(self):
        with self.nc.Block() as block:
            @block.sync
            def _(e):
                for fn in self.q["sync"]:
                    fn(e)

            @block.vector
            def _(e):
                for fn in self.q["vector"]:
                    fn(e)

            @block.scalar
            def _(e):
                for fn in self.q["scalar"]:
                    fn(e)

            @block.tensor
            def _(e):
                for fn in self.q["tensor"]:
                    fn(e)

            @block.gpsimd
            def _(e):
                for fn in self.q["gpsimd"]:
                    fn(e)


def make_consts():
    c = {}
    c["c_ident_bf"] = np.eye(128, dtype=np.float32)
    c["c_ident_f32"] = np.eye(128, dtype=np.float32)
    c["c_ones_row"] = np.ones((1, 128), np.float32)
    c["c_ones_col"] = np.ones((128, 1), np.float32)
    jj, ii = np.meshgrid(np.arange(128), np.arange(128), indexing="ij")
    c["c_tri_ji"] = (jj <= ii).astype(np.float32)
    c["c_ones_row512"] = np.ones((1, 512), np.float32)
    c["c_iota8"] = np.broadcast_to(np.arange(8, dtype=np.float32), (128, 8)).copy()
    sel = np.zeros((8, 8 * 128), np.float32)
    for e in range(8):
        sel[e, e * 128:(e + 1) * 128] = 1.0
    c["c_sel8"] = sel
    c["c_iota32"] = np.broadcast_to(np.arange(32, dtype=np.float32), (128, 32)).copy()
    c["c_tri_cum"] = c["c_tri_ji"].copy()
    jj32, ii32 = np.meshgrid(np.arange(32), np.arange(32), indexing="ij")
    c["c_tri32s"] = (jj32 < ii32).astype(np.float32)
    c["c_thr32"] = np.broadcast_to(128.0 * np.arange(32, dtype=np.float32) + 0.5,
                                   (32, 32)).copy()
    c["c_k64"] = (np.arange(64, dtype=np.float32) + 0.5).reshape(64, 1)
    c["c_iotatok"] = np.arange(128, dtype=np.float32).reshape(128, 1)
    G = np.zeros((65, 320), np.float32)
    for m in range(320):
        r = int(np.clip(m - 158, -32, 32) + 32)
        G[r, m] = 1.0
    c["c_relG"] = G
    return c


def declare_consts(nc):
    consts = make_consts()
    return {k: nc.dram_tensor(k, list(v.shape), F32, kind="ExternalInput").ap()
            for k, v in consts.items()}


def emit_weight_transpose(P, sems, w_dram, wt_sb, wnat, ident_bf, tp_ps,
                          tp_free, wt_ap=None, extra_wait=0):
    gld, t2v, vv, v2t = sems
    wdst = wt_ap if wt_ap is not None else wt_sb[:]
    ldv = P.emit("gpsimd",
                 lambda e, w=w_dram, wn=wnat:
                 e.dma_start(out=wn.rearrange("p (oc i) -> p oc i", oc=NFC),
                             in_=w.rearrange("(oc p) i -> p oc i", p=128)),
                 wait=[(vv, max(P.wnat_free, extra_wait))], inc=(gld, 16))
    last_t = None
    for oc in range(NFC):
        for ic in range(NFC):
            k = oc * NFC + ic
            par = k % 2
            pt = tp_ps[par]
            tv = P.emit("tensor",
                        lambda e, wn=wnat, oc=oc, ic=ic, pt=pt:
                        e.transpose(pt[:, 0:128],
                                    wn[:, oc * DM + ic * 128: oc * DM + (ic + 1) * 128],
                                    ident_bf[:]),
                        wait=[(gld, ldv), (vv, tp_free[par])],
                        inc=(t2v, 1))
            ev = P.emit("vector",
                        lambda e, wt=wdst, oc=oc, ic=ic, pt=pt:
                        e.tensor_copy(wt[:, ic * DM + oc * 128: ic * DM + (oc + 1) * 128],
                                      pt[:, 0:128]),
                        wait=[(t2v, tv)], inc=(vv, 1))
            tp_free[par] = ev
            last_t = ev
    P.wnat_free = last_t


def emit_ln(P, sems, x_ap, scr, xhat_out_ap, pre, dump_ap=None):
    ss, vv = sems
    if dump_ap is None:
        dump_ap = xhat_out_ap
    P.emit("scalar",
           lambda e: e.activation(dump_ap, x_ap, AF.Square,
                                  accum_out=scr["sx2"][:]),
           wait=pre)
    s2 = P.emit("scalar",
                lambda e: e.activation(dump_ap, x_ap, AF.Copy,
                                       accum_out=scr["sx"][:]),
                inc=(ss, 1))
    v1 = P.emit("vector",
                lambda e: e.tensor_scalar(scr["m"][:], scr["sx"][:], 1.0 / DM,
                                          None, ALU.mult),
                wait=[(ss, s2)], inc=(vv, 1))
    v2 = P.emit("vector",
                lambda e: e.tensor_scalar(scr["mm"][:], scr["m"][:], scr["m"][:],
                                          None, ALU.mult),
                wait=[(vv, v1)], inc=(vv, 1))
    v3 = P.emit("vector",
                lambda e: e.tensor_scalar(scr["var"][:], scr["sx2"][:], 1.0 / DM,
                                          scr["mm"][:], ALU.mult, ALU.subtract),
                wait=[(vv, v2)], inc=(vv, 1))
    s3 = P.emit("scalar",
                lambda e: e.activation(scr["lnv"][:], scr["var"][:], AF.Ln,
                                       bias=scr["eps"][:]),
                wait=[(vv, v3)], inc=(ss, 1))
    s4 = P.emit("scalar",
                lambda e: e.activation(scr["rstd"][:], scr["lnv"][:], AF.Exp,
                                       scale=-0.5),
                wait=[(ss, s3)], inc=(ss, 1))
    v4 = P.emit("vector",
                lambda e, xo=xhat_out_ap: e.tensor_scalar(
                    xo, x_ap, scr["m"][:], scr["rstd"][:], ALU.subtract, ALU.mult),
                wait=[(ss, s4)], inc=(vv, 1))
    return v4


def ln_scratch(P, tag):
    scr = {
        "sx": P.sb(f"lnsx_{tag}", [128, 1], F32),
        "sx2": P.sb(f"lnsx2_{tag}", [128, 1], F32),
        "m": P.sb(f"lnm_{tag}", [128, 1], F32),
        "mm": P.sb(f"lnmm_{tag}", [128, 1], F32),
        "var": P.sb(f"lnvar_{tag}", [128, 1], F32),
        "lnv": P.sb(f"lnlnv_{tag}", [128, 1], F32),
        "rstd": P.sb(f"lnrstd_{tag}", [128, 1], F32),
        "eps": P.sb(f"lneps_{tag}", [128, 1], F32),
    }
    return scr



def build_l0_attn():
    nc = bass.Bass("TRN2", target_bir_lowering=False, debug=False, num_devices=8)
    xin = nc.dram_tensor("xin", [TOK, DM], F32, kind="ExternalInput").ap()
    wts = {n: nc.dram_tensor(n, [DM, DM], F32, kind="ExternalInput").ap()
           for n in ("wq", "wk", "wv", "wo")}
    bo = nc.dram_tensor("bo", [1, DM], F32, kind="ExternalInput").ap()
    cin = declare_consts(nc)
    xout = nc.dram_tensor("xout", [TOK, DM], F32, kind="ExternalOutput").ap()

    with ExitStack() as ctx:
        P = PB(nc, ctx)
        ident_bf = P.sb("ident_bf", [128, 128], BF16)
        ones_row = P.sb("ones_row", [1, 128], BF16)
        ones_col = P.sb("ones_col", [128, 1], BF16)
        tri_ji = P.sb("tri_ji", [128, 128], F32)
        bo_sb = P.sb("bo_sb", [1, DM], BF16)
        wt = {n: P.sb(f"wt_{n}", [128, NFC * DM], BF16)
              for n in ("wq", "wk", "wv")}
        tp_ps = [P.ps(f"tp_ps{i}", [128, 128], BF16) for i in range(2)]
        ld = P.sem("ld")
        gld = P.sem("gld")
        t2v = P.sem("t2v")
        ss = P.sem("ss")
        vv = P.sem("vv")
        st = P.sem("st")
        P.wnat_free = 0
        wnat_guard = nc.sbuf_tensor("wnat", [128, NFC * DM], BF16, side="right")
        wnat = wnat_guard.__enter__()
        for name, sb_t in (("c_ident_bf", ident_bf), ("c_ones_row", ones_row),
                           ("c_ones_col", ones_col), ("c_tri_ji", tri_ji)):
            P.emit("gpsimd", lambda e, s=sb_t, n=name: e.dma_start(out=s[:], in_=cin[n]),
                   inc=(gld, 16))
        P.emit("gpsimd", lambda e: e.dma_start(out=bo_sb[:], in_=bo), inc=(gld, 16))
        tp_free = [0, 0]
        for n in ("wq", "wk", "wv"):
            emit_weight_transpose(P, (gld, t2v, vv, None), wts[n], wt[n], wnat,
                                  ident_bf, tp_ps, tp_free)
        wnat_guard.__exit__(None, None, None)
        xtile = [P.sb(f"xtile{i}", [128, DM], F32) for i in range(2)]
        lns = ln_scratch(P, "a")
        P.emit("vector", lambda e: e.memset(lns["eps"][:], 1e-5), inc=(vv, 1))
        xhat_tm = P.sb("xhat_tm", [128, DM], BF16)
        xhat_fm = P.sb("xhat_fm", [128, NFC * SEQ], BF16)
        q_fm = P.sb("q_fm", [128, NFC * SEQ], BF16)
        k_fm = P.sb("k_fm", [128, NFC * SEQ], BF16)
        v_tm = P.sb("v_tm", [128, NCH * DM], BF16)
        phi_min = P.sb("phi_min", [128, 512], BF16)
        phi_exp = P.sb("phi_exp", [128, 512], F32)
        a_sb = [P.sb(f"a_sb{i}", [128, 128], BF16) for i in range(2)]
        den_f = [P.sb(f"den_f{i}", [1, 128], F32) for i in range(2)]
        rec_bf = [P.sb(f"rec_bf{i}", [1, 128], BF16) for i in range(2)]
        bden = [P.sb(f"bden{i}", [128, 128], F32) for i in range(2)]
        pk_tm = [P.sb(f"pk_tm{i}", [128, 64], BF16) for i in range(2)]
        s_bf_all = P.sb("s_bf_all", [128, (NH // 2) * HD], BF16)
        z_f32 = P.sb("z_f32", [128, NH], F32)
        z_bf = P.sb("z_bf", [128, NH], BF16)
        zadd = P.sb("zadd", [128, 1], F32)
        rtile = xtile
        ao_fm = xhat_fm

        comb = [P.ps(f"comb{i}", [128, 512], F32) for i in range(2)]
        s_ps = [P.ps(f"s_ps{i}", [128, 512], F32) for i in range(2)]
        mm_ps = [P.ps(f"mm_ps{i}", [128, 512], F32) for i in range(2)]

        mm_free = [0, 0]
        mm_free_sem = [vv, vv]

        for b in range(NB):
            xhat_vv = 0
            ln_done = []
            for tt in range(NCH):
                xb = xtile[tt % 2]
                w_pre = ([(vv, ln_done[tt - 2])] if tt >= 2
                         else [(vv, P.wnat_free), (st, st.n), (vv, vv.n)])
                ldv = P.emit("sync",
                             lambda e, xb=xb, row0=b * SEQ + tt * 128:
                             e.dma_start(out=xb[:], in_=xin[row0:row0 + 128, :]),
                             wait=w_pre, inc=(ld, 16))
                v4 = emit_ln(P, (ss, vv), xb[:], lns, xhat_tm[:],
                             pre=[(ld, ldv), (vv, ln_done[tt - 1] if tt >= 1 else 0),
                                  (vv, 1)])
                last_ev = 0
                for fc in range(NFC):
                    par = fc % 2
                    tv = P.emit("tensor",
                                lambda e, pt=tp_ps[par], fc=fc:
                                e.transpose(pt[:, 0:128],
                                            xhat_tm[:, fc * 128:(fc + 1) * 128],
                                            ident_bf[:]),
                                wait=[(vv, v4), (vv, tp_free[par])],
                                inc=(t2v, 1))
                    ev = P.emit("vector",
                                lambda e, pt=tp_ps[par], fc=fc, tt=tt:
                                e.tensor_copy(
                                    xhat_fm[:, fc * SEQ + tt * 128: fc * SEQ + (tt + 1) * 128],
                                    pt[:, 0:128]),
                                wait=[(t2v, tv)], inc=(vv, 1))
                    tp_free[par] = ev
                    last_ev = ev
                ln_done.append(last_ev)
            XHAT_VV = ln_done[-1]

            qkv_ev = []
            for (dst, wname) in ((q_fm, "wq"), (k_fm, "wk")):
                for oc in range(NFC):
                    for tg in range(SEQ // 512):
                        par = len(qkv_ev) % 2
                        pb_ = mm_ps[par]
                        pairs = [(wt[wname][:, kc * DM + oc * 128: kc * DM + (oc + 1) * 128],
                                  xhat_fm[:, kc * SEQ + tg * 512: kc * SEQ + tg * 512 + 512])
                                 for kc in range(NFC)]
                        P.mm_group(pb_[:], pairs,
                                   wait=[(vv, XHAT_VV),
                                         (mm_free_sem[par], mm_free[par]),
                                         (ss, qkv_ev[-2] if len(qkv_ev) >= 2 else 0)],
                                   inc=(t2v, 1))
                        tvv = t2v.n
                        ev = P.emit("scalar",
                                    lambda e, pb_=pb_, dst=dst, oc=oc, tg=tg:
                                    e.activation(
                                        dst[:, oc * SEQ + tg * 512: oc * SEQ + tg * 512 + 512],
                                        pb_[:], AF.Copy),
                                    wait=[(t2v, tvv)], inc=(ss, 1))
                        qkv_ev.append(ev)
                        mm_free[par] = ev
                        mm_free_sem[par] = ss
            for tt in range(NCH):
                for og in range(2):
                    par = len(qkv_ev) % 2
                    pb_ = mm_ps[par]
                    pairs = [(xhat_fm[:, kc * SEQ + tt * 128: kc * SEQ + (tt + 1) * 128],
                              wt["wv"][:, kc * DM + og * 512: kc * DM + og * 512 + 512])
                             for kc in range(NFC)]
                    P.mm_group(pb_[:], pairs,
                               wait=[(vv, XHAT_VV),
                                     (mm_free_sem[par], mm_free[par])],
                               inc=(t2v, 1))
                    tvv = t2v.n
                    ev = P.emit("scalar",
                                lambda e, pb_=pb_, tt=tt, og=og:
                                e.activation(
                                    v_tm[:, tt * DM + og * 512: tt * DM + og * 512 + 512],
                                    pb_[:], AF.Copy),
                                wait=[(t2v, tvv)], inc=(ss, 1))
                    qkv_ev.append(ev)
                    mm_free[par] = ev
                    mm_free_sem[par] = ss
            QKV_SS = ss.n

            prev_s1 = 0
            last_phi = 0
            for src in (q_fm, k_fm):
                for u in range(NFC * SEQ // 512):
                    sl = src[:, u * 512:(u + 1) * 512]
                    v1_ = P.emit("vector",
                                 lambda e, sl=sl: e.tensor_single_scalar(
                                     phi_min[:], sl, 0.0, ALU.min),
                                 wait=[(ss, max(QKV_SS, prev_s1))], inc=(vv, 1))
                    prev_s1 = P.emit("scalar",
                                     lambda e: e.activation(phi_exp[:], phi_min[:],
                                                            AF.Exp),
                                     wait=[(vv, v1_)], inc=(ss, 1))
                    last_phi = P.emit("vector",
                                      lambda e, sl=sl: e.scalar_tensor_tensor(
                                          sl, sl, 0.0, phi_exp[:], ALU.max, ALU.add),
                                      wait=[(ss, prev_s1)], inc=(vv, 1))
            PHI_VV = last_phi

            zi = P.emit("vector", lambda e: e.memset(z_f32[:], 0.0),
                        wait=[(vv, PHI_VV)], inc=(vv, 1))
            zi2 = P.emit("vector", lambda e: e.memset(z_bf[:], 0.0),
                         wait=[(vv, zi)], inc=(vv, 1))

            head_sV = [0] * NH
            head_zV = [zi2] * NH
            comb_free = [0, 0]
            sbank_free = [0, 0]
            for c in range(NCH):
                for h in range(NH):
                    par = h % 2
                    fc, ro = h // 2, 64 * (h % 2)
                    pq = q_fm[ro:ro + 64, fc * SEQ + c * 128: fc * SEQ + (c + 1) * 128]
                    pk = k_fm[ro:ro + 64, fc * SEQ + c * 128: fc * SEQ + (c + 1) * 128]
                    vt = v_tm[:, c * DM + h * HD: c * DM + (h + 1) * HD]
                    sreg = s_bf_all[ro:ro + 64, (h // 2) * HD:(h // 2) * HD + HD]
                    cb = comb[par]
                    tv_at = P.emit("tensor",
                                   lambda e, pk=pk, pq=pq, cb=cb:
                                   e.matmul(cb[:, 0:128], pk, pq, start=True,
                                            stop=True, skip_group_check=True),
                                   wait=[(vv, max(PHI_VV, comb_free[par]))],
                                   inc=(t2v, 1))
                    v_mask = P.emit("vector",
                                    lambda e, cb=cb, d=a_sb[par]:
                                    e.tensor_tensor(d[:], cb[:, 0:128], tri_ji[:],
                                                    ALU.mult),
                                    wait=[(t2v, tv_at)], inc=(vv, 1))
                    tv_den = P.mm_group(
                        cb[0:1, 128:256],
                        [(ones_col[:], a_sb[par][:]),
                         (z_bf[ro:ro + 64, h:h + 1], pq)],
                        wait=[(vv, max(v_mask, head_zV[h]))],
                        inc=(t2v, 1), skip_group_check=True)
                    v_d1 = P.emit("vector",
                                  lambda e, cb=cb, d=den_f[par]:
                                  e.tensor_scalar(d[:], cb[0:1, 128:256], 1e-5,
                                                  None, ALU.add),
                                  wait=[(t2v, tv_den)], inc=(vv, 1))
                    v_d2 = P.emit("vector",
                                  lambda e, d=den_f[par]: e.reciprocal(d[:], d[:]),
                                  wait=[(vv, v_d1)], inc=(vv, 1))
                    v_d3 = P.emit("vector",
                                  lambda e, d=den_f[par], r=rec_bf[par]:
                                  e.tensor_copy(r[:], d[:]),
                                  wait=[(vv, v_d2)], inc=(vv, 1))
                    tv_bb = P.emit("tensor",
                                   lambda e, r=rec_bf[par], cb=cb, ro=ro:
                                   e.matmul(cb[ro:ro + 64, 256:384],
                                            ones_row[0:1, 0:64],
                                            r[:], start=True, stop=True,
                                            skip_group_check=True),
                                   wait=[(vv, v_d3)], inc=(t2v, 1))
                    v_bb = P.emit("vector",
                                  lambda e, cb=cb, d=bden[par], ro=ro:
                                  e.tensor_copy(d[ro:ro + 64, :],
                                                cb[ro:ro + 64, 256:384]),
                                  wait=[(t2v, tv_bb)], inc=(vv, 1))
                    pairs = [(vt, a_sb[par][:])]
                    if c > 0:
                        pairs.append((sreg, pq))
                    tv_num = P.mm_group(cb[ro:ro + 64, 384:512], pairs,
                                        wait=[(vv, max(v_bb, head_sV[h]))],
                                        inc=(t2v, 1), skip_group_check=True)
                    v_out = P.emit("vector",
                                   lambda e, cb=cb, bd=bden[par], ro=ro, fc=fc, c=c:
                                   e.tensor_tensor(
                                       ao_fm[ro:ro + 64,
                                             fc * SEQ + c * 128: fc * SEQ + (c + 1) * 128],
                                       cb[ro:ro + 64, 384:512], bd[ro:ro + 64, :],
                                       ALU.mult),
                                   wait=[(t2v, tv_num)], inc=(vv, 1))
                    comb_free[par] = v_out
                    tv_pkt = P.emit("tensor",
                                    lambda e, pk=pk, p=tp_ps[par], ro=ro:
                                    e.transpose(p[0:128, 0:64], pk,
                                                ident_bf[ro:ro + 64, ro:ro + 64]),
                                    wait=[(vv, tp_free[par])], inc=(t2v, 1))
                    v_pkt = P.emit("vector",
                                   lambda e, p=tp_ps[par], d=pk_tm[par]:
                                   e.tensor_copy(d[:], p[0:128, 0:64]),
                                   wait=[(t2v, tv_pkt)], inc=(vv, 1))
                    tp_free[par] = v_pkt
                    srgn = s_ps[par][ro:ro + 64, (h // 2) * HD:(h // 2) * HD + HD]
                    tv_sup = P.emit("tensor",
                                    lambda e, pkt=pk_tm[par], vt=vt, srgn=srgn,
                                    st_=(c == 0 and h == par):
                                    e.matmul(srgn, pkt[:], vt, start=st_,
                                             stop=False, skip_group_check=True),
                                    wait=[(vv, max(v_pkt, sbank_free[par]))],
                                    inc=(t2v, 1))
                    v_se = P.emit("vector",
                                  lambda e, srgn=srgn, d=sreg:
                                  e.tensor_copy(d, srgn),
                                  wait=[(t2v, tv_sup)], inc=(vv, 1))
                    head_sV[h] = v_se
                    sbank_free[par] = v_se
                    v_z1 = P.emit("vector",
                                  lambda e, pk=pk, ro=ro:
                                  e.reduce_sum(zadd[ro:ro + 64, :], pk, axis=AX.X),
                                  wait=[(vv, head_zV[h])], inc=(vv, 1))
                    v_z2 = P.emit("vector",
                                  lambda e, h=h, ro=ro: e.tensor_tensor(
                                      z_f32[ro:ro + 64, h:h + 1],
                                      z_f32[ro:ro + 64, h:h + 1],
                                      zadd[ro:ro + 64, :], ALU.add),
                                  wait=[(vv, v_z1)], inc=(vv, 1))
                    v_z3 = P.emit("vector",
                                  lambda e, h=h, ro=ro, tvd=tv_den: e.tensor_copy(
                                      z_bf[ro:ro + 64, h:h + 1],
                                      z_f32[ro:ro + 64, h:h + 1]),
                                  wait=[(vv, v_z2), (t2v, tv_den)], inc=(vv, 1))
                    head_zV[h] = v_z3
            AO_VV = vv.n

            wt_o = v_tm[:, 0:NFC * DM]
            wnat2 = v_tm[:, NFC * DM:2 * NFC * DM]
            P.wnat_free = AO_VV
            emit_weight_transpose(P, (gld, t2v, vv, None), wts["wo"], None,
                                  wnat2, ident_bf, tp_ps, tp_free,
                                  wt_ap=wt_o, extra_wait=AO_VV)

            res_free = [0, 0]
            res_st = [0, 0]
            for tt in range(NCH):
                rt = rtile[tt % 2]
                ot = rt
                ldv = P.emit("sync",
                             lambda e, rt=rt, row0=b * SEQ + tt * 128:
                             e.dma_start(out=rt[:], in_=xin[row0:row0 + 128, :]),
                             wait=[(vv, res_free[tt % 2]),
                                   (st, res_st[tt % 2])], inc=(ld, 16))
                tt_vv = []
                for og in range(2):
                    par = og
                    pb_ = mm_ps[par]
                    pairs = [(ao_fm[:, kc * SEQ + tt * 128: kc * SEQ + (tt + 1) * 128],
                              wt_o[:, kc * DM + og * 512: kc * DM + og * 512 + 512])
                             for kc in range(NFC)]
                    P.mm_group(pb_[:], pairs,
                               wait=[(vv, AO_VV), (mm_free_sem[par], mm_free[par])],
                               first_start=True, last_stop=False)
                    tvv = P.emit("tensor",
                                 lambda e, pb_=pb_, og=og:
                                 e.matmul(pb_[:], ones_row[:],
                                          bo_sb[0:1, og * 512: og * 512 + 512],
                                          start=False, stop=True),
                                 inc=(t2v, 1))
                    v_r = P.emit("vector",
                                 lambda e, pb_=pb_, rt=rt, ot=ot, og=og:
                                 e.tensor_tensor(ot[:, og * 512:og * 512 + 512],
                                                 pb_[:],
                                                 rt[:, og * 512:og * 512 + 512],
                                                 ALU.add),
                                 wait=[(t2v, tvv), (ld, ldv)], inc=(vv, 1))
                    tt_vv.append(v_r)
                    mm_free[par] = v_r
                    mm_free_sem[par] = vv
                stv = P.emit("sync",
                             lambda e, ot=ot, row0=b * SEQ + tt * 128:
                             e.dma_start(out=xout[row0:row0 + 128, :], in_=ot[:]),
                             wait=[(vv, tt_vv[-1])], inc=(st, 16))
                res_free[tt % 2] = tt_vv[-1]
                res_st[tt % 2] = stv
        P.raw("sync", lambda e, n=st.n: e.wait_ge(st.h, n))
        P.finish()
    return nc





def build_l0_ffn():
    nc = bass.Bass("TRN2", target_bir_lowering=False, debug=False, num_devices=8)
    xin = nc.dram_tensor("xin", [TOK, DM], F32, kind="ExternalInput").ap()
    gate_w = nc.dram_tensor("gate_w", [DM, 8], F32, kind="ExternalInput").ap()
    gbias = nc.dram_tensor("gbias", [1, 8], F32, kind="ExternalInput").ap()
    up_w = nc.dram_tensor("up_w", [8, DM, 4096], F32, kind="ExternalInput").ap()
    up_b = nc.dram_tensor("up_b", [8, 4096], F32, kind="ExternalInput").ap()
    down_w = nc.dram_tensor("down_w", [8, 4096, DM], F32, kind="ExternalInput").ap()
    down_b = nc.dram_tensor("down_b", [8, DM], F32, kind="ExternalInput").ap()
    sw_up = nc.dram_tensor("sw_up", [DM, 4096], F32, kind="ExternalInput").ap()
    sb_up = nc.dram_tensor("sb_up", [1, 4096], F32, kind="ExternalInput").ap()
    sw_down = nc.dram_tensor("sw_down", [4096, DM], F32, kind="ExternalInput").ap()
    sb_down = nc.dram_tensor("sb_down", [1, DM], F32, kind="ExternalInput").ap()
    cin = declare_consts(nc)
    xout = nc.dram_tensor("xout", [TOK, DM], F32, kind="ExternalOutput").ap()

    NTG = 4
    TGT = 1024
    NIC = 32

    with ExitStack() as ctx:
        P = PB(nc, ctx)
        ident_bf = P.sb("ident_bf", [128, 128], BF16)
        ident_f32 = P.sb("ident_f32", [128, 128], F32)
        ones_row = P.sb("ones_row", [1, 128], BF16)
        ones_row512 = P.sb("ones_row512", [1, 512], BF16)
        iota8 = P.sb("iota8", [128, 8], F32)
        sel8 = P.sb("sel8", [8, 8 * 128], BF16)
        gate_sb = P.sb("gate_sb", [128, NFC * 8], BF16)
        gbias_sb = P.sb("gbias_sb", [1, 8], BF16)
        down_b_all = P.sb("down_b_all", [8, DM], BF16)
        sb_down01 = P.sb("sb_down01", [1, DM], F32)
        sb_down01b = P.sb("sb_down01b", [1, DM], BF16)
        upb_row = P.sb("upb_row", [1, 4096], BF16)
        lns = ln_scratch(P, "b")
        xtile = [P.sb(f"xtile{i}", [128, DM], F32) for i in range(2)]
        xhat_tm = P.sb("xhat_tm", [128, DM], BF16)
        xhat_fm = P.sb("xhat_fm", [128, NFC * TOK], BF16)
        hact = P.sb("hact", [128, NIC * TGT], BF16)
        comb = P.sb("comb", [128, 8 * DM], F32)
        sc_sb = P.sb("sc_sb", [128, 8], F32)
        s1_sb = P.sb("s1_sb", [128, 1], F32)
        s2_sb = P.sb("s2_sb", [128, 1], F32)
        i1_sb = P.sb("i1_sb", [128, 1], F32)
        i2_sb = P.sb("i2_sb", [128, 1], F32)
        m1_sb = P.sb("m1_sb", [128, 8], F32)
        m2_sb = P.sb("m2_sb", [128, 8], F32)
        tmp8 = P.sb("tmp8", [128, 8], F32)
        tmp8b = P.sb("tmp8b", [128, 8], F32)
        rec_sb = P.sb("rec_sb", [128, 1], F32)
        q1_sb = P.sb("q1_sb", [128, 1], F32)
        q2_sb = P.sb("q2_sb", [128, 1], F32)
        w01 = P.sb("w01", [128, 8], F32)
        w01T = P.sb("w01T", [8, TOK], BF16)
        upslab = [P.sb(f"upslab{i}", [128, DM], BF16) for i in range(2)]
        dnslab = [P.sb(f"dnslab{i}", [128, 512], BF16) for i in range(2)]
        wbc = P.sb("wbc", [128, TGT], BF16)
        mnorm = P.sb("mnorm", [128, 1], F32)
        rnorm = P.sb("rnorm", [128, 1], F32)

        ld = P.sem("ld")
        gld = P.sem("gld")
        t2v = P.sem("t2v")
        ss = P.sem("ss")
        vv = P.sem("vv")
        st = P.sem("st")
        P.wnat_free = 0

        tpg = [nc.psum_tensor(f"tp_ps{i}", [128, 128], BF16) for i in range(2)]
        scg = nc.psum_tensor("sc_ps", [128, 512], F32)
        tp_ps = [g.__enter__() for g in tpg]
        sc_ps = scg.__enter__()
        tp_free = [0, 0]

        for name, sb_t in (("c_ident_bf", ident_bf), ("c_ident_f32", ident_f32),
                           ("c_ones_row", ones_row),
                           ("c_ones_row512", ones_row512), ("c_iota8", iota8),
                           ("c_sel8", sel8)):
            P.emit("gpsimd", lambda e, s=sb_t, n=name: e.dma_start(out=s[:], in_=cin[n]),
                   inc=(gld, 16))
        P.emit("gpsimd", lambda e: e.dma_start(
            out=gate_sb.rearrange("p (kc e) -> p kc e", kc=NFC),
            in_=gate_w.rearrange("(kc p) e -> p kc e", p=128)), inc=(gld, 16))
        P.emit("gpsimd", lambda e: e.dma_start(out=gbias_sb[:], in_=gbias), inc=(gld, 16))
        P.emit("gpsimd", lambda e: e.dma_start(out=down_b_all[:], in_=down_b), inc=(gld, 16))
        P.emit("gpsimd", lambda e: e.dma_start(out=sb_down01[:], in_=sb_down), inc=(gld, 16))
        LDC = gld.n
        P.emit("vector", lambda e: e.memset(lns["eps"][:], 1e-5), inc=(vv, 1))
        P.emit("vector", lambda e: e.memset(comb[:], 0.0), inc=(vv, 1))
        CMB0 = vv.n
        s01 = P.emit("scalar", lambda e: e.activation(sb_down01b[:], sb_down01[:],
                                                      AF.Copy, scale=0.1),
                     wait=[(gld, LDC)], inc=(ss, 1))

        ln_done = []
        for tt in range(NTT):
            xb = xtile[tt % 2]
            w_pre = [(vv, ln_done[tt - 2])] if tt >= 2 else [(gld, LDC)]
            ldv = P.emit("sync",
                         lambda e, xb=xb, row0=tt * 128:
                         e.dma_start(out=xb[:], in_=xin[row0:row0 + 128, :]),
                         wait=w_pre, inc=(ld, 16))
            v4 = emit_ln(P, (ss, vv), xb[:], lns, xhat_tm[:],
                         pre=[(ld, ldv), (vv, ln_done[tt - 1] if tt >= 1 else 1)])
            last_ev = 0
            for fc in range(NFC):
                par = fc % 2
                tv = P.emit("tensor",
                            lambda e, pt=tp_ps[par], fc=fc:
                            e.transpose(pt[:, 0:128],
                                        xhat_tm[:, fc * 128:(fc + 1) * 128],
                                        ident_bf[:]),
                            wait=[(vv, v4), (vv, tp_free[par])], inc=(t2v, 1))
                ev = P.emit("vector",
                            lambda e, pt=tp_ps[par], fc=fc, tt=tt:
                            e.tensor_copy(
                                xhat_fm[:, fc * TOK + tt * 128: fc * TOK + (tt + 1) * 128],
                                pt[:, 0:128]),
                            wait=[(t2v, tv)], inc=(vv, 1))
                tp_free[par] = ev
                last_ev = ev
            ln_done.append(last_ev)
        XH_VV = ln_done[-1]

        sc_free = (vv, 0)
        for tt in range(NTT):
            pairs = [(xhat_fm[:, kc * TOK + tt * 128: kc * TOK + (tt + 1) * 128],
                      gate_sb[:, kc * 8:(kc + 1) * 8]) for kc in range(NFC)]
            P.mm_group(sc_ps[:, 0:8], pairs,
                       wait=[(vv, ln_done[tt]), (gld, LDC),
                             (sc_free[0], sc_free[1])],
                       first_start=True, last_stop=False, skip_group_check=True)
            tv1 = P.emit("tensor",
                         lambda e: e.matmul(sc_ps[:, 0:8], ones_row[:], gbias_sb[:],
                                            start=False, stop=True,
                                            skip_group_check=True),
                         inc=(t2v, 1))
            sg = P.emit("scalar",
                        lambda e: e.activation(sc_sb[:], sc_ps[:, 0:8], AF.Sigmoid),
                        wait=[(t2v, tv1)], inc=(ss, 1))
            a1 = P.emit("vector", lambda e: e.reduce_max(s1_sb[:], sc_sb[:], axis=AX.X),
                        wait=[(ss, sg)], inc=(vv, 1))
            a2 = P.emit("vector", lambda e: e.tensor_scalar(m1_sb[:], sc_sb[:],
                                                            s1_sb[:], None, ALU.is_equal),
                        wait=[(vv, a1)], inc=(vv, 1))
            a3 = P.emit("vector", lambda e: e.tensor_tensor(tmp8[:], iota8[:], m1_sb[:],
                                                            ALU.mult),
                        wait=[(vv, a2)], inc=(vv, 1))
            a4 = P.emit("vector", lambda e: e.scalar_tensor_tensor(
                tmp8b[:], m1_sb[:], -32768.0, tmp8[:], ALU.mult, ALU.add),
                wait=[(vv, a3)], inc=(vv, 1))
            a5 = P.emit("vector", lambda e: e.tensor_reduce(i1_sb[:], tmp8b[:],
                                                            axis=AX.X, op=ALU.min),
                        wait=[(vv, a4)], inc=(vv, 1))
            a6 = P.emit("vector", lambda e: e.tensor_scalar(i1_sb[:], i1_sb[:],
                                                            32768.0, None, ALU.add),
                        wait=[(vv, a5)], inc=(vv, 1))
            a7 = P.emit("vector", lambda e: e.tensor_scalar(m1_sb[:], iota8[:],
                                                            i1_sb[:], None, ALU.is_equal),
                        wait=[(vv, a6)], inc=(vv, 1))
            b1 = P.emit("vector", lambda e: e.scalar_tensor_tensor(
                tmp8[:], m1_sb[:], -10.0, sc_sb[:], ALU.mult, ALU.add),
                wait=[(vv, a7)], inc=(vv, 1))
            b2 = P.emit("vector", lambda e: e.reduce_max(s2_sb[:], tmp8[:], axis=AX.X),
                        wait=[(vv, b1)], inc=(vv, 1))
            b3 = P.emit("vector", lambda e: e.tensor_scalar(m2_sb[:], tmp8[:],
                                                            s2_sb[:], None, ALU.is_equal),
                        wait=[(vv, b2)], inc=(vv, 1))
            b4 = P.emit("vector", lambda e: e.tensor_tensor(tmp8b[:], iota8[:], m2_sb[:],
                                                            ALU.mult),
                        wait=[(vv, b3)], inc=(vv, 1))
            b5 = P.emit("vector", lambda e: e.scalar_tensor_tensor(
                tmp8[:], m2_sb[:], -32768.0, tmp8b[:], ALU.mult, ALU.add),
                wait=[(vv, b4)], inc=(vv, 1))
            b6 = P.emit("vector", lambda e: e.tensor_reduce(i2_sb[:], tmp8[:],
                                                            axis=AX.X, op=ALU.min),
                        wait=[(vv, b5)], inc=(vv, 1))
            b7 = P.emit("vector", lambda e: e.tensor_scalar(i2_sb[:], i2_sb[:],
                                                            32768.0, None, ALU.add),
                        wait=[(vv, b6)], inc=(vv, 1))
            b8 = P.emit("vector", lambda e: e.tensor_scalar(m2_sb[:], iota8[:],
                                                            i2_sb[:], None, ALU.is_equal),
                        wait=[(vv, b7)], inc=(vv, 1))
            c1 = P.emit("vector", lambda e: e.tensor_scalar(rec_sb[:], s1_sb[:],
                                                            s2_sb[:], 1e-6, ALU.add, ALU.add),
                        wait=[(vv, b8)], inc=(vv, 1))
            c2 = P.emit("vector", lambda e: e.reciprocal(rec_sb[:], rec_sb[:]),
                        wait=[(vv, c1)], inc=(vv, 1))
            c3 = P.emit("vector", lambda e: e.tensor_scalar(q1_sb[:], rec_sb[:],
                                                            s1_sb[:], 0.1, ALU.mult, ALU.mult),
                        wait=[(vv, c2)], inc=(vv, 1))
            c4 = P.emit("vector", lambda e: e.tensor_scalar(q2_sb[:], rec_sb[:],
                                                            s2_sb[:], 0.1, ALU.mult, ALU.mult),
                        wait=[(vv, c3)], inc=(vv, 1))
            c6 = P.emit("vector", lambda e: e.tensor_scalar(tmp8[:], m1_sb[:],
                                                            q1_sb[:], None, ALU.mult),
                        wait=[(vv, c4)], inc=(vv, 1))
            c7 = P.emit("vector", lambda e: e.scalar_tensor_tensor(
                w01[:], m2_sb[:], q2_sb[:], tmp8[:], ALU.mult, ALU.add),
                wait=[(vv, c6)], inc=(vv, 1))
            tvw = P.emit("tensor",
                         lambda e: e.transpose(sc_ps[0:8, 8:136], w01[:],
                                               ident_f32[:]),
                         wait=[(vv, c7), (ss, sg)], inc=(t2v, 1))
            evw = P.emit("vector",
                         lambda e, tt=tt: e.tensor_copy(
                             w01T[:, tt * 128:(tt + 1) * 128], sc_ps[0:8, 8:136]),
                         wait=[(t2v, tvw)], inc=(vv, 1))
            sc_free = (vv, evw)
        W01_VV = vv.n
        P2_T = t2v.n

        scg.__exit__(None, None, None)
        for g in reversed(tpg):
            g.__exit__(None, None, None)
        dn_ps = [P.ps(f"dn_ps{i}", [128, 512], F32) for i in range(8)]
        bank_free = [(vv, 0)] * 8
        comb_slice_v = [[0, 0] for _ in range(8)]

        comb_evac = CMB0
        for tg in range(NTG):
            comb_ready = []
            for e_ in range(9):
                shared = (e_ == 8)
                ub = P.emit("gpsimd",
                            lambda e, e_=e_, sh=shared:
                            e.dma_start(out=upb_row[:],
                                        in_=(sb_up if sh else up_b[e_:e_ + 1, :])),
                            wait=[(t2v, getattr(P, "_upb_t", 0))],
                            inc=(gld, 16))
                if not shared:
                    wb_evs = []
                    for half in range(2):
                        bfree = bank_free[2]
                        tvb = P.emit("tensor",
                                     lambda e, e_=e_, tg=tg, half=half:
                                     e.matmul(dn_ps[2][:, 0:512],
                                              sel8[:, e_ * 128:(e_ + 1) * 128],
                                              w01T[:, tg * TGT + half * 512:
                                                   tg * TGT + half * 512 + 512],
                                              start=True, stop=True,
                                              skip_group_check=True),
                                     wait=[(vv, W01_VV), (bfree[0], bfree[1])],
                                     inc=(t2v, 1))
                        evb = P.emit("vector",
                                     lambda e, half=half: e.tensor_copy(
                                         wbc[:, half * 512: half * 512 + 512],
                                         dn_ps[2][:, 0:512]),
                                     wait=[(t2v, tvb)], inc=(vv, 1))
                        wb_evs.append(evb)
                        bank_free[2] = (vv, evb)
                    WBC_VV = wb_evs[-1]
                else:
                    WBC_VV = 0
                up_evs = []
                for ic in range(NIC):
                    sl = upslab[ic % 2]
                    ldu = P.emit("gpsimd",
                                 lambda e, sl=sl, e_=e_, ic=ic, sh=shared:
                                 e.dma_start(
                                     out=sl.rearrange("p (kc i) -> p kc i", kc=NFC),
                                     in_=(sw_up if sh else up_w[e_]).rearrange(
                                         "(kc p) i -> p kc i", p=128)[:, :, ic * 128:(ic + 1) * 128]),
                                 wait=[(t2v, up_evs[-2][1]) if len(up_evs) >= 2 else (t2v, 0)],
                                 inc=(gld, 16))
                    for half in range(2):
                        bank = dn_ps[half]
                        bfree = bank_free[half]
                        pairs = [(sl[:, kc * 128:(kc + 1) * 128],
                                  xhat_fm[:, kc * TOK + tg * TGT + half * 512:
                                          kc * TOK + tg * TGT + half * 512 + 512])
                                 for kc in range(NFC)]
                        P.mm_group(bank[:], pairs,
                                   wait=[(gld, ldu), (bfree[0], bfree[1]),
                                         (vv, WBC_VV), (vv, XH_VV)],
                                   first_start=True, last_stop=False,
                                   skip_group_check=True)
                        tvu = P.emit("tensor",
                                     lambda e, bank=bank, ic=ic:
                                     e.matmul(bank[:], upb_row[0:1, ic * 128:(ic + 1) * 128],
                                              ones_row512[:], start=False, stop=True,
                                              skip_group_check=True),
                                     wait=[(gld, ub)], inc=(t2v, 1))
                        sg2 = P.emit("scalar",
                                     lambda e, bank=bank, ic=ic, half=half:
                                     e.activation(
                                         hact[:, ic * TGT + half * 512: ic * TGT + half * 512 + 512],
                                         bank[:], AF.Gelu),
                                     wait=[(t2v, tvu)], inc=(ss, 1))
                        if shared:
                            vsc = P.emit("vector",
                                         lambda e, ic=ic, half=half:
                                         e.tensor_scalar(
                                             hact[:, ic * TGT + half * 512: ic * TGT + half * 512 + 512],
                                             hact[:, ic * TGT + half * 512: ic * TGT + half * 512 + 512],
                                             0.1, None, ALU.mult),
                                         wait=[(ss, sg2)], inc=(vv, 1))
                        else:
                            vsc = P.emit("vector",
                                         lambda e, ic=ic, half=half:
                                         e.tensor_tensor(
                                             hact[:, ic * TGT + half * 512: ic * TGT + half * 512 + 512],
                                             hact[:, ic * TGT + half * 512: ic * TGT + half * 512 + 512],
                                             wbc[:, half * 512: half * 512 + 512],
                                             ALU.mult),
                                         wait=[(ss, sg2)], inc=(vv, 1))
                        bank_free[half] = (ss, sg2)
                        up_evs.append((vsc, t2v.n))
                UP_VV = up_evs[-1][0]
                P._upb_t = t2v.n
                for hg in range(2):
                    dn_last = []
                    for ic in range(NIC):
                        dsl = dnslab[ic % 2]
                        ldd = P.emit("gpsimd",
                                     lambda e, dsl=dsl, e_=e_, ic=ic, hg=hg, sh=shared:
                                     e.dma_start(out=dsl[:],
                                                 in_=(sw_down if sh else down_w[e_])[
                                                     ic * 128:(ic + 1) * 128,
                                                     hg * 512: hg * 512 + 512]),
                                     wait=[(t2v, dn_last[-2]) if len(dn_last) >= 2 else (t2v, 0)],
                                     inc=(gld, 16))
                        for tt in range(8):
                            w_ = [(gld, ldd), (vv, UP_VV)]
                            if ic == 0:
                                w_.append(bank_free[tt])
                            tvd = P.emit("tensor",
                                         lambda e, dsl=dsl, ic=ic, tt=tt, st_=(ic == 0):
                                         e.matmul(dn_ps[tt][:],
                                                  hact[:, ic * TGT + tt * 128: ic * TGT + (tt + 1) * 128],
                                                  dsl[:], start=st_, stop=False,
                                                  skip_group_check=True),
                                         wait=w_, inc=(t2v, 1) if tt == 7 else None)
                        dn_last.append(t2v.n)
                    if shared:
                        for tt in range(8):
                            P.emit("tensor",
                                   lambda e, tt=tt, tg=tg, hg=hg:
                                   e.matmul(dn_ps[tt][:],
                                            w01T[:, (tg * 8 + tt) * 128:(tg * 8 + tt + 1) * 128],
                                            down_b_all[:, hg * 512: hg * 512 + 512],
                                            start=False, stop=False,
                                            skip_group_check=True))
                            P.emit("tensor",
                                   lambda e, tt=tt, hg=hg:
                                   e.matmul(dn_ps[tt][:], ones_row[:],
                                            sb_down01b[0:1, hg * 512: hg * 512 + 512],
                                            start=False, stop=True,
                                            skip_group_check=True),
                                   wait=[(ss, s01)],
                                   inc=(t2v, 1) if tt == 7 else None)
                        dn_last.append(t2v.n)
                    DN_T = t2v.n
                    for tt in range(8):
                        cev = P.emit("vector",
                                     lambda e, tt=tt, hg=hg:
                                     e.tensor_tensor(
                                         comb[:, tt * DM + hg * 512: tt * DM + hg * 512 + 512],
                                         dn_ps[tt][:],
                                         comb[:, tt * DM + hg * 512: tt * DM + hg * 512 + 512],
                                         ALU.add),
                                     wait=[(t2v, DN_T),
                                           (vv, max(comb_evac, comb_slice_v[tt][hg]))],
                                     inc=(vv, 1))
                        bank_free[tt] = (vv, cev)
                        comb_slice_v[tt][hg] = cev
                comb_last = vv.n
            st_free = [0, 0]
            res_free = [0, 0]
            for tt in range(8):
                rt = xtile[tt % 2]
                gtt = tg * 8 + tt
                ldv = P.emit("sync",
                             lambda e, rt=rt, row0=(tg * 8 + tt) * 128:
                             e.dma_start(out=rt[:], in_=xin[row0:row0 + 128, :]),
                             wait=[(vv, res_free[tt % 2]), (st, st_free[tt % 2])],
                             inc=(ld, 16))
                n1 = P.emit("vector",
                            lambda e, tt=tt: e.tensor_reduce(
                                mnorm[:], comb[:, tt * DM:(tt + 1) * DM], axis=AX.X,
                                op=ALU.max, apply_absolute_value=True),
                            wait=[(vv, comb_last)], inc=(vv, 1))
                n2 = P.emit("vector",
                            lambda e: e.tensor_scalar(rnorm[:], mnorm[:], 1e-6,
                                                      None, ALU.add),
                            wait=[(vv, n1)], inc=(vv, 1))
                n3 = P.emit("vector",
                            lambda e: e.reciprocal(rnorm[:], rnorm[:]),
                            wait=[(vv, n2)], inc=(vv, 1))
                n4 = P.emit("vector",
                            lambda e, rt=rt, tt=tt: e.scalar_tensor_tensor(
                                rt[:], comb[:, tt * DM:(tt + 1) * DM], rnorm[:],
                                rt[:], ALU.mult, ALU.add),
                            wait=[(vv, n3), (ld, ldv)], inc=(vv, 1))
                stv = P.emit("sync",
                             lambda e, rt=rt, row0=(tg * 8 + tt) * 128:
                             e.dma_start(out=xout[row0:row0 + 128, :], in_=rt[:]),
                             wait=[(vv, n4)], inc=(st, 16))
                res_free[tt % 2] = n4
                st_free[tt % 2] = stv
            if tg < NTG - 1:
                comb_evac = P.emit("vector", lambda e: e.memset(comb[:], 0.0),
                                   wait=[(vv, vv.n)], inc=(vv, 1))
                comb_slice_v = [[0, 0] for _ in range(8)]
        P.raw("sync", lambda e, n=st.n: e.wait_ge(st.h, n))
        P.finish()
    return nc


def run_l0_ffn(inputs, x1_full):
    nc = build_l0_ffn()
    consts = make_consts()
    shards = _shard_x(x1_full)
    in_maps = []
    for c in range(NCORES):
        m = {"xin": shards[c],
             "gate_w": np.ascontiguousarray(inputs["L0_ffn_gate_w"]),
             "gbias": np.ascontiguousarray(inputs["L0_ffn_bias"].reshape(1, 8)),
             "up_w": np.ascontiguousarray(inputs["L0_ffn_up_w"]),
             "up_b": np.ascontiguousarray(inputs["L0_ffn_up_b"]),
             "down_w": np.ascontiguousarray(inputs["L0_ffn_down_w"]),
             "down_b": np.ascontiguousarray(inputs["L0_ffn_down_b"]),
             "sw_up": np.ascontiguousarray(inputs["L0_ffn_sw_up"]),
             "sb_up": np.ascontiguousarray(inputs["L0_ffn_sb_up"].reshape(1, 4096)),
             "sw_down": np.ascontiguousarray(inputs["L0_ffn_sw_down"]),
             "sb_down": np.ascontiguousarray(inputs["L0_ffn_sb_down"].reshape(1, DM))}
        m.update(consts)
        in_maps.append(m)
    res = run_bass_kernel_spmd(nc, in_maps, core_ids=list(range(NCORES)))
    return _unshard_x([res.results[c]["xout"] for c in range(NCORES)])





def build_l1_attn():
    nc = bass.Bass("TRN2", target_bir_lowering=False, debug=False, num_devices=8)
    xin = nc.dram_tensor("xin", [TOK, DM], F32, kind="ExternalInput").ap()
    wts = {n: nc.dram_tensor(n, [DM, DM], F32, kind="ExternalInput").ap()
           for n in ("wq", "wk", "wv", "wo")}
    bo = nc.dram_tensor("bo", [1, DM], F32, kind="ExternalInput").ap()
    relk = nc.dram_tensor("relk", [65, HD], F32, kind="ExternalInput").ap()
    relv = nc.dram_tensor("relv", [65, HD], F32, kind="ExternalInput").ap()
    cin = declare_consts(nc)
    xout = nc.dram_tensor("xout", [TOK, DM], F32, kind="ExternalOutput").ap()
    srow_d = nc.dram_tensor("srow_d", [128 * 320 + 512], F32, kind="Internal").ap()
    band_d = nc.dram_tensor("band_d", [128 * 1153 + 16], BF16, kind="Internal").ap()
    BW = 1152

    with ExitStack() as ctx:
        P = PB(nc, ctx)
        ident_bf = P.sb("ident_bf", [128, 128], BF16)
        ident_f32 = P.sb("ident_f32", [128, 128], F32)
        ones_row = P.sb("ones_row", [1, 128], BF16)
        bo_sb = P.sb("bo_sb", [1, DM], BF16)
        relG = P.sb("relG", [65, 320], BF16)
        relk_sb = P.sb("relk_sb", [65, HD], BF16)
        relv_sb = P.sb("relv_sb", [65, HD], BF16)
        relkT = P.sb("relkT", [128, 65], BF16)
        wtbuf = [P.sb(f"wtbuf{i}", [128, NFC * DM], BF16) for i in range(2)]
        tp_ps = [P.ps(f"tp_ps{i}", [128, 128], BF16) for i in range(2)]
        mm_ps = [P.ps(f"mm_ps{i}", [128, 512], F32) for i in range(2)]
        cx_ps = P.ps("cx_ps", [128, 512], F32)
        ld = P.sem("ld")
        gld = P.sem("gld")
        t2v = P.sem("t2v")
        ss = P.sem("ss")
        vv = P.sem("vv")
        st = P.sem("st")
        P.wnat_free = 0

        for name, sb_t in (("c_ident_bf", ident_bf), ("c_ident_f32", ident_f32),
                           ("c_ones_row", ones_row), ("c_relG", relG)):
            P.emit("gpsimd", lambda e, s=sb_t, n=name: e.dma_start(out=s[:], in_=cin[n]),
                   inc=(gld, 16))
        P.emit("gpsimd", lambda e: e.dma_start(out=bo_sb[:], in_=bo), inc=(gld, 16))
        P.emit("gpsimd", lambda e: e.dma_start(out=relk_sb[:], in_=relk), inc=(gld, 16))
        P.emit("gpsimd", lambda e: e.dma_start(out=relv_sb[:], in_=relv), inc=(gld, 16))
        LDC = gld.n
        tp_free = [0, 0]
        for half in range(2):
            tvr = P.emit("tensor",
                         lambda e, half=half:
                         e.transpose(tp_ps[0][64 * half:64 * half + 64, 0:65],
                                     relk_sb[:], ident_bf[0:65, 0:65]),
                         wait=[(gld, LDC), (vv, tp_free[0])], inc=(t2v, 1))
            ev = P.emit("vector",
                        lambda e, half=half:
                        e.tensor_copy(relkT[64 * half:64 * half + 64, :],
                                      tp_ps[0][64 * half:64 * half + 64, 0:65]),
                        wait=[(t2v, tvr)], inc=(vv, 1))
            tp_free[0] = ev

        xtile = [P.sb(f"xtile{i}", [128, DM], F32) for i in range(2)]
        lns = ln_scratch(P, "c")
        xhat_tm = P.sb("xhat_tm", [128, DM], BF16)
        xhat_fm = P.sb("xhat_fm", [128, NFC * SEQ], BF16)
        q_fm = P.sb("q_fm", [128, NFC * SEQ], BF16)
        k_fm = P.sb("k_fm", [128, NFC * SEQ], BF16)
        v_tm = P.sb("v_tm", [128, NCH * DM], BF16)
        ao_fm = xhat_fm
        qrT = P.sb("qrT", [65, SEQ], BF16)
        qr_tm = P.sb("qr_tm", [128, 65], F32)
        srow_sb = P.sb("srow_sb", [128, 320], F32)
        wwin = P.sb("wwin", [128, 192], F32)
        scores = P.sb("scores", [128, SEQ], F32)
        expq = P.sb("expq", [128, SEQ], BF16)
        bext = P.sb("bext", [128, 1024], BF16)
        ar_sb = P.sb("ar_sb", [128, 65], F32)
        arT_bf = P.sb("arT_bf", [65, 128], BF16)
        expT = P.sb("expT", [128, 128], BF16)
        nm_sb = P.sb("nm_sb", [128, 1], F32)
        zs_sb = P.sb("zs_sb", [128, 1], F32)
        zr_bf = P.sb("zr_bf", [128, 1], BF16)
        zrow = P.sb("zrow", [1, 128], BF16)
        zbc = P.sb("zbc", [128, 128], F32)
        zeros_sb = P.sb("zeros_sb", [128, 512], BF16)

        P.emit("vector", lambda e: e.memset(lns["eps"][:], 1e-5), inc=(vv, 1))
        zv = P.emit("vector", lambda e: e.memset(zeros_sb[:], 0.0), inc=(vv, 1))
        g1 = P.emit("sync",
                    lambda e: e.dma_start(
                        out=band_d[0:128 * BW].rearrange("(r x) -> r x", x=BW)[:, 0:128],
                        in_=zeros_sb[:, 0:128]),
                    wait=[(vv, zv)], inc=(ld, 16))
        g2 = P.emit("sync",
                    lambda e: e.dma_start(
                        out=band_d[0:128 * BW].rearrange("(r x) -> r x", x=BW)[:, 1024:1152],
                        in_=zeros_sb[:, 0:128]),
                    inc=(ld, 16))
        GUARD_LD = ld.n

        for b in range(NB):
            ln_done = []
            for tt in range(NCH):
                xb = xtile[tt % 2]
                w_pre = ([(vv, ln_done[tt - 2])] if tt >= 2
                         else [(vv, P.wnat_free), (st, st.n), (vv, vv.n)])
                ldv = P.emit("sync",
                             lambda e, xb=xb, row0=b * SEQ + tt * 128:
                             e.dma_start(out=xb[:], in_=xin[row0:row0 + 128, :]),
                             wait=w_pre, inc=(ld, 16))
                v4 = emit_ln(P, (ss, vv), xb[:], lns, xhat_tm[:],
                             pre=[(ld, ldv), (vv, ln_done[tt - 1] if tt >= 1 else 1)])
                last_ev = 0
                for fc in range(NFC):
                    par = fc % 2
                    tv = P.emit("tensor",
                                lambda e, pt=tp_ps[par], fc=fc:
                                e.transpose(pt[:, 0:128],
                                            xhat_tm[:, fc * 128:(fc + 1) * 128],
                                            ident_bf[:]),
                                wait=[(vv, v4), (vv, tp_free[par])], inc=(t2v, 1))
                    ev = P.emit("vector",
                                lambda e, pt=tp_ps[par], fc=fc, tt=tt:
                                e.tensor_copy(
                                    xhat_fm[:, fc * SEQ + tt * 128: fc * SEQ + (tt + 1) * 128],
                                    pt[:, 0:128]),
                                wait=[(t2v, tv)], inc=(vv, 1))
                    tp_free[par] = ev
                    last_ev = ev
                ln_done.append(last_ev)
            XHAT_VV = ln_done[-1]

            qkv_ev = []
            mm_free = [(vv, 0), (vv, 0)]
            wnat_ap = v_tm[:, 0:NFC * DM]
            prev_batch_vv = getattr(P, "_batch_done_vv", 0)
            for pi, (dst, wname, scl) in enumerate(
                    ((q_fm, "wq", 0.125), (k_fm, "wk", 1.0), (v_tm, "wv", 1.0))):
                wdst = wtbuf[pi % 2]
                emit_weight_transpose(P, (gld, t2v, vv, None), wts[wname], wdst,
                                      wnat_ap, ident_bf, tp_ps, tp_free,
                                      extra_wait=prev_batch_vv)
                WT_VV = P.wnat_free
                if wname != "wv":
                    for oc in range(NFC):
                        for tg in range(SEQ // 512):
                            par = len(qkv_ev) % 2
                            pb_ = mm_ps[par]
                            pairs = [(wdst[:, kc * DM + oc * 128: kc * DM + (oc + 1) * 128],
                                      xhat_fm[:, kc * SEQ + tg * 512: kc * SEQ + tg * 512 + 512])
                                     for kc in range(NFC)]
                            P.mm_group(pb_[:], pairs,
                                       wait=[(vv, max(XHAT_VV, WT_VV)), mm_free[par]],
                                       inc=(t2v, 1))
                            tvv = t2v.n
                            ev = P.emit("scalar",
                                        lambda e, pb_=pb_, dst=dst, oc=oc, tg=tg, scl=scl:
                                        e.activation(
                                            dst[:, oc * SEQ + tg * 512: oc * SEQ + tg * 512 + 512],
                                            pb_[:], AF.Copy, scale=scl),
                                        wait=[(t2v, tvv)], inc=(ss, 1))
                            qkv_ev.append(ev)
                            mm_free[par] = (ss, ev)
                else:
                    for tt in range(NCH):
                        for og in range(2):
                            par = len(qkv_ev) % 2
                            pb_ = mm_ps[par]
                            pairs = [(xhat_fm[:, kc * SEQ + tt * 128: kc * SEQ + (tt + 1) * 128],
                                      wdst[:, kc * DM + og * 512: kc * DM + og * 512 + 512])
                                     for kc in range(NFC)]
                            P.mm_group(pb_[:], pairs,
                                       wait=[(vv, max(XHAT_VV, WT_VV)), mm_free[par]],
                                       inc=(t2v, 1))
                            tvv = t2v.n
                            ev = P.emit("scalar",
                                        lambda e, pb_=pb_, tt=tt, og=og:
                                        e.activation(
                                            v_tm[:, tt * DM + og * 512: tt * DM + og * 512 + 512],
                                            pb_[:], AF.Copy),
                                        wait=[(t2v, tvv)], inc=(ss, 1))
                            qkv_ev.append(ev)
                            mm_free[par] = (ss, ev)
            QKV_SS = ss.n

            srow_free = (vv, 0)
            band_free = (vv, 0)
            for h in range(NH):
                fc, ro = h // 2, 64 * (h % 2)
                qr_evs = []
                for cg in range(SEQ // 512):
                    par = cg % 2
                    tvq = P.emit("tensor",
                                 lambda e, cg=cg, ro=ro, fc=fc, pb_=mm_ps[par]:
                                 e.matmul(pb_[0:65, 0:512],
                                          relkT[ro:ro + 64, :],
                                          q_fm[ro:ro + 64, fc * SEQ + cg * 512:
                                               fc * SEQ + cg * 512 + 512],
                                          start=True, stop=True,
                                          skip_group_check=True),
                                 wait=[(ss, QKV_SS), mm_free[par]],
                                 inc=(t2v, 1))
                    evq = P.emit("vector",
                                 lambda e, cg=cg, pb_=mm_ps[par]:
                                 e.tensor_copy(qrT[:, cg * 512: cg * 512 + 512],
                                               pb_[0:65, 0:512]),
                                 wait=[(t2v, tvq)], inc=(vv, 1))
                    mm_free[par] = (vv, evq)
                    qr_evs.append(evq)
                QR_VV = qr_evs[-1]

                for c in range(NCH):
                    i0 = c * 128
                    sc_evs = []
                    for cg in range(4):
                        par = cg % 2
                        tvs = P.emit("tensor",
                                     lambda e, c=c, cg=cg, ro=ro, fc=fc, i0=i0, pb_=mm_ps[par]:
                                     e.matmul(pb_[:],
                                              q_fm[ro:ro + 64, fc * SEQ + i0: fc * SEQ + i0 + 128],
                                              k_fm[ro:ro + 64, fc * SEQ + cg * 512:
                                                   fc * SEQ + cg * 512 + 512],
                                              start=True, stop=True,
                                              skip_group_check=True),
                                     wait=[(vv, QR_VV), mm_free[par],
                                           (vv, getattr(P, "_scores_free", 0))],
                                     inc=(t2v, 1))
                        evs_ = P.emit("scalar",
                                      lambda e, cg=cg, pb_=mm_ps[par]:
                                      e.activation(scores[:, cg * 512: cg * 512 + 512],
                                                   pb_[:], AF.Copy),
                                      wait=[(t2v, tvs)], inc=(ss, 1))
                        mm_free[par] = (ss, evs_)
                        sc_evs.append(evs_)
                    SC_SS = sc_evs[-1]
                    par = 0
                    tvsr = P.emit("tensor",
                                  lambda e, c=c, i0=i0, pb_=mm_ps[par]:
                                  e.matmul(pb_[:, 0:320],
                                           qrT[:, i0:i0 + 128], relG[:],
                                           start=True, stop=True,
                                           skip_group_check=True),
                                  wait=[(vv, QR_VV), mm_free[par]],
                                  inc=(t2v, 1))
                    evsr = P.emit("vector",
                                  lambda e, pb_=mm_ps[par]:
                                  e.tensor_copy(srow_sb[:], pb_[:, 0:320]),
                                  wait=[(t2v, tvsr), srow_free], inc=(vv, 1))
                    mm_free[par] = (vv, evsr)
                    dsw = P.emit("sync",
                                 lambda e: e.dma_start(
                                     out=srow_d[0:128 * 320].rearrange("(r x) -> r x", x=320),
                                     in_=srow_sb[:]),
                                 wait=[(vv, evsr)], inc=(ld, 16))
                    dsr = P.emit("sync",
                                 lambda e: e.dma_start(
                                     out=wwin[:, 0:190],
                                     in_=srow_d[127:127 + 128 * 319].rearrange(
                                         "(r x) -> r x", x=319)[:, 0:190]),
                                 wait=[(ld, dsw)], inc=(ld, 16))
                    tvqt = P.emit("tensor",
                                  lambda e, c=c, i0=i0, pt=tp_ps[0]:
                                  e.transpose(pt[:, 0:65], qrT[:, i0:i0 + 128],
                                              ident_bf[0:65, 0:65]),
                                  wait=[(vv, tp_free[0])], inc=(t2v, 1))
                    evqt = P.emit("vector",
                                  lambda e, pt=tp_ps[0]:
                                  e.tensor_copy(qr_tm[:], pt[:, 0:65]),
                                  wait=[(t2v, tvqt)], inc=(vv, 1))
                    tp_free[0] = evqt
                    lw = max(0, i0 - 31)
                    rw = min(SEQ, i0 + 159)
                    wa = lw - (i0 - 31)
                    adds = []
                    if lw > 0:
                        adds.append(P.emit("vector",
                                           lambda e, lw=lw: e.tensor_scalar(
                                               scores[:, 0:lw], scores[:, 0:lw],
                                               qr_tm[:, 0:1], None, ALU.add),
                                           wait=[(ss, SC_SS), (vv, evqt)], inc=(vv, 1)))
                    if rw < SEQ:
                        adds.append(P.emit("vector",
                                           lambda e, rw=rw: e.tensor_scalar(
                                               scores[:, rw:SEQ], scores[:, rw:SEQ],
                                               qr_tm[:, 64:65], None, ALU.add),
                                           wait=[(ss, SC_SS), (vv, evqt)], inc=(vv, 1)))
                    aw = P.emit("vector",
                                lambda e, lw=lw, rw=rw, wa=wa: e.tensor_tensor(
                                    scores[:, lw:rw], scores[:, lw:rw],
                                    wwin[:, wa:wa + (rw - lw)], ALU.add),
                                wait=[(ss, SC_SS), (ld, dsr)], inc=(vv, 1))
                    adds.append(aw)
                    vm = P.emit("vector",
                                lambda e: e.reduce_max(nm_sb[:], scores[:], axis=AX.X,
                                                       negate=True),
                                wait=[(vv, max(adds))], inc=(vv, 1))
                    P._scores_free = vm
                    se = P.emit("scalar",
                                lambda e: e.activation(expq[:], scores[:], AF.Exp,
                                                       bias=nm_sb[:],
                                                       accum_out=zs_sb[:]),
                                wait=[(vv, vm), band_free], inc=(ss, 1))
                    dls = []
                    J0 = i0 - 384
                    wlo, whi = max(0, J0), min(SEQ, i0 + 512)
                    if J0 < 0:
                        dls.append(P.emit("sync",
                                          lambda e, w=-J0: e.dma_start(
                                              out=band_d[0:128 * BW].rearrange(
                                                  "(r x) -> r x", x=BW)[:, 128:128 + w],
                                              in_=zeros_sb[:, 0:w]),
                                          wait=[(ss, se), (ld, GUARD_LD)], inc=(ld, 16)))
                    if i0 + 512 > SEQ:
                        wzr = i0 + 512 - SEQ
                        dls.append(P.emit("sync",
                                          lambda e, w=wzr: e.dma_start(
                                              out=band_d[0:128 * BW].rearrange(
                                                  "(r x) -> r x", x=BW)[:, 1024 - w:1024],
                                              in_=zeros_sb[:, 0:w]),
                                          wait=[(ss, se), (ld, GUARD_LD)], inc=(ld, 16)))
                    dbw = P.emit("sync",
                                 lambda e, wlo=wlo, whi=whi, J0=J0: e.dma_start(
                                     out=band_d[0:128 * BW].rearrange(
                                         "(r x) -> r x", x=BW)[:, 128 + wlo - J0:128 + whi - J0],
                                     in_=expq[:, wlo:whi]),
                                 wait=[(ss, se)] + [(ld, d) for d in dls], inc=(ld, 16))
                    dbr = P.emit("sync",
                                 lambda e: e.dma_start(
                                     out=bext[:, 0:1023],
                                     in_=band_d[1:1 + 128 * 1153].rearrange(
                                         "(r x) -> r x", x=1153)[:, 0:1023]),
                                 wait=[(ld, dbw)], inc=(ld, 16))
                    a0 = P.emit("vector",
                                lambda e: e.reduce_sum(ar_sb[:, 0:1], bext[:, 0:480],
                                                       axis=AX.X),
                                wait=[(ld, dbr)], inc=(vv, 1))
                    a64 = P.emit("vector",
                                 lambda e: e.reduce_sum(ar_sb[:, 64:65],
                                                        bext[:, 543:1023], axis=AX.X),
                                 wait=[(vv, a0)], inc=(vv, 1))
                    mid = P.emit("vector",
                                 lambda e: e.tensor_copy(ar_sb[:, 1:64],
                                                         bext[:, 480:543]),
                                 wait=[(vv, a64)], inc=(vv, 1))
                    lastar = mid
                    if i0 - 384 > 0:
                        r1 = P.emit("vector",
                                    lambda e, J0=J0: e.reduce_sum(nm_sb[:],
                                                                  expq[:, 0:J0], axis=AX.X),
                                    wait=[(vv, mid)], inc=(vv, 1))
                        r2 = P.emit("vector",
                                    lambda e: e.tensor_tensor(ar_sb[:, 0:1],
                                                              ar_sb[:, 0:1], nm_sb[:],
                                                              ALU.add),
                                    wait=[(vv, r1)], inc=(vv, 1))
                        lastar = r2
                    if i0 + 512 < SEQ:
                        r3 = P.emit("vector",
                                    lambda e, i0=i0: e.reduce_sum(
                                        nm_sb[:], expq[:, i0 + 512:SEQ], axis=AX.X),
                                    wait=[(vv, lastar)], inc=(vv, 1))
                        r4 = P.emit("vector",
                                    lambda e: e.tensor_tensor(ar_sb[:, 64:65],
                                                              ar_sb[:, 64:65], nm_sb[:],
                                                              ALU.add),
                                    wait=[(vv, r3)], inc=(vv, 1))
                        lastar = r4
                    tvar = P.emit("tensor",
                                  lambda e, pt=tp_ps[1]:
                                  e.transpose(mm_ps[1][0:65, 0:128], ar_sb[:],
                                              ident_f32[:]),
                                  wait=[(vv, lastar), mm_free[1]], inc=(t2v, 1))
                    evar = P.emit("vector",
                                  lambda e: e.tensor_copy(arT_bf[:], mm_ps[1][0:65, 0:128]),
                                  wait=[(t2v, tvar)], inc=(vv, 1))
                    mm_free[1] = (vv, evar)
                    exp_free = (vv, 0)
                    for kt in range(NCH):
                        tvt = P.emit("tensor",
                                     lambda e, kt=kt, pt=tp_ps[1]:
                                     e.transpose(pt[:, 0:128],
                                                 expq[:, kt * 128:(kt + 1) * 128],
                                                 ident_bf[:]),
                                     wait=[(vv, tp_free[1]), (ss, se)], inc=(t2v, 1))
                        evt = P.emit("vector",
                                     lambda e, pt=tp_ps[1]:
                                     e.tensor_copy(expT[:], pt[:, 0:128]),
                                     wait=[(t2v, tvt), exp_free], inc=(vv, 1))
                        tp_free[1] = evt
                        tvc = P.emit("tensor",
                                     lambda e, kt=kt, c=c, ro=ro, h=h, st_=(kt == 0):
                                     e.matmul(cx_ps[ro:ro + 64, 0:128],
                                              v_tm[:, kt * DM + h * HD: kt * DM + (h + 1) * HD],
                                              expT[:], start=st_, stop=False,
                                              skip_group_check=True),
                                     wait=[(vv, evt)] + ([(vv, getattr(P, "_cx_free", 0))]
                                                         if kt == 0 else []),
                                     inc=(t2v, 1))
                        exp_free = (t2v, tvc)
                    tvrv = P.emit("tensor",
                                  lambda e, ro=ro:
                                  e.matmul(cx_ps[ro:ro + 64, 0:128], relv_sb[:],
                                           arT_bf[:], start=False, stop=True,
                                           skip_group_check=True),
                                  wait=[(vv, evar)], inc=(t2v, 1))
                    z1 = P.emit("vector",
                                lambda e: e.reciprocal(zs_sb[:], zs_sb[:]),
                                wait=[(ss, se)], inc=(vv, 1))
                    z2 = P.emit("vector",
                                lambda e: e.tensor_copy(zr_bf[:], zs_sb[:]),
                                wait=[(vv, z1)], inc=(vv, 1))
                    tvz = P.emit("tensor",
                                 lambda e, pt=tp_ps[0]:
                                 e.transpose(pt[0:1, 0:128], zr_bf[:], ident_bf[:]),
                                 wait=[(vv, z2), (vv, tp_free[0])], inc=(t2v, 1))
                    evz = P.emit("vector",
                                 lambda e, pt=tp_ps[0]:
                                 e.tensor_copy(zrow[:], pt[0:1, 0:128]),
                                 wait=[(t2v, tvz)], inc=(vv, 1))
                    tp_free[0] = evz
                    tvb = P.emit("tensor",
                                 lambda e, ro=ro, pb_=mm_ps[0]:
                                 e.matmul(pb_[ro:ro + 64, 0:128],
                                          ones_row[0:1, 0:64], zrow[:],
                                          start=True, stop=True,
                                          skip_group_check=True),
                                 wait=[(vv, evz), mm_free[0]], inc=(t2v, 1))
                    evb = P.emit("vector",
                                 lambda e, ro=ro, pb_=mm_ps[0]:
                                 e.tensor_copy(zbc[ro:ro + 64, :], pb_[ro:ro + 64, 0:128]),
                                 wait=[(t2v, tvb)], inc=(vv, 1))
                    mm_free[0] = (vv, evb)
                    vao = P.emit("vector",
                                 lambda e, c=c, ro=ro, fc=fc, i0=i0: e.tensor_tensor(
                                     ao_fm[ro:ro + 64, fc * SEQ + i0: fc * SEQ + i0 + 128],
                                     cx_ps[ro:ro + 64, 0:128], zbc[ro:ro + 64, :],
                                     ALU.mult),
                                 wait=[(t2v, tvrv), (vv, evb)], inc=(vv, 1))
                    srow_free = (vv, vao)
                    band_free = (vv, vao)
                    P._cx_free = vao
            AO_VV = vv.n

            wt_o = v_tm[:, 0:NFC * DM]
            wnat2 = v_tm[:, NFC * DM:2 * NFC * DM]
            P.wnat_free = AO_VV
            emit_weight_transpose(P, (gld, t2v, vv, None), wts["wo"], None,
                                  wnat2, ident_bf, tp_ps, tp_free,
                                  wt_ap=wt_o, extra_wait=AO_VV)

            res_free = [0, 0]
            res_st = [0, 0]
            for tt in range(NCH):
                rt = xtile[tt % 2]
                ldv = P.emit("sync",
                             lambda e, rt=rt, row0=b * SEQ + tt * 128:
                             e.dma_start(out=rt[:], in_=xin[row0:row0 + 128, :]),
                             wait=[(vv, res_free[tt % 2]), (st, res_st[tt % 2])],
                             inc=(ld, 16))
                tt_vv = []
                for og in range(2):
                    par = og
                    pb_ = mm_ps[par]
                    pairs = [(ao_fm[:, kc * SEQ + tt * 128: kc * SEQ + (tt + 1) * 128],
                              wt_o[:, kc * DM + og * 512: kc * DM + og * 512 + 512])
                             for kc in range(NFC)]
                    P.mm_group(pb_[:], pairs,
                               wait=[(vv, P.wnat_free), mm_free[par]],
                               first_start=True, last_stop=False)
                    tvv = P.emit("tensor",
                                 lambda e, pb_=pb_, og=og:
                                 e.matmul(pb_[:], ones_row[:],
                                          bo_sb[0:1, og * 512: og * 512 + 512],
                                          start=False, stop=True),
                                 wait=[(gld, LDC)], inc=(t2v, 1))
                    v_r = P.emit("vector",
                                 lambda e, pb_=pb_, rt=rt, og=og:
                                 e.tensor_tensor(rt[:, og * 512:og * 512 + 512],
                                                 pb_[:],
                                                 rt[:, og * 512:og * 512 + 512],
                                                 ALU.add),
                                 wait=[(t2v, tvv), (ld, ldv)], inc=(vv, 1))
                    tt_vv.append(v_r)
                    mm_free[par] = (vv, v_r)
                stv = P.emit("sync",
                             lambda e, rt=rt, row0=b * SEQ + tt * 128:
                             e.dma_start(out=xout[row0:row0 + 128, :], in_=rt[:]),
                             wait=[(vv, tt_vv[-1])], inc=(st, 16))
                res_free[tt % 2] = tt_vv[-1]
                res_st[tt % 2] = stv
            P._batch_done_vv = vv.n
        P.raw("sync", lambda e, n=st.n: e.wait_ge(st.h, n))
        P.finish()
    return nc


def run_l1_attn(inputs, x2_full):
    nc = build_l1_attn()
    consts = make_consts()
    shards = _shard_x(x2_full)
    in_maps = []
    for c in range(NCORES):
        m = {"xin": shards[c],
             "wq": np.ascontiguousarray(inputs["L1_attn_Wq"]),
             "wk": np.ascontiguousarray(inputs["L1_attn_Wk"]),
             "wv": np.ascontiguousarray(inputs["L1_attn_Wv"]),
             "wo": np.ascontiguousarray(inputs["L1_attn_Wo"]),
             "bo": np.ascontiguousarray(inputs["L1_attn_b_o"].reshape(1, DM)),
             "relk": np.ascontiguousarray(inputs["L1_attn_rel_k_emb"]),
             "relv": np.ascontiguousarray(inputs["L1_attn_rel_v_emb"])}
        m.update(consts)
        in_maps.append(m)
    res = run_bass_kernel_spmd(nc, in_maps, core_ids=list(range(NCORES)))
    return _unshard_x([res.results[c]["xout"] for c in range(NCORES)])





CAP = 1280


def _emit_router(P, sems, cin_aps, bufs, with_pos):
    pass


def build_l1_route():
    nc = bass.Bass("TRN2", target_bir_lowering=False, debug=False, num_devices=8)
    xin = nc.dram_tensor("xin", [TOK, DM], F32, kind="ExternalInput").ap()
    wr = nc.dram_tensor("wr", [DM, 32], F32, kind="ExternalInput").ap()
    cin = declare_consts(nc)
    counts = nc.dram_tensor("counts", [1, 32], F32, kind="ExternalOutput").ap()

    with ExitStack() as ctx:
        P = PB(nc, ctx)
        ident_bf = P.sb("ident_bf", [128, 128], BF16)
        ones_col = P.sb("ones_col", [128, 1], BF16)
        iota32 = P.sb("iota32", [128, 32], F32)
        wr_sb = P.sb("wr_sb", [128, NFC * 32], BF16)
        xtile = [P.sb(f"xtile{i}", [128, DM], F32) for i in range(2)]
        lns = ln_scratch(P, "r")
        xhat_tm = P.sb("xhat_tm", [128, DM], BF16)
        xhat_fm = P.sb("xhat_fm", [128, NFC * 128], BF16)
        lg_sb = P.sb("lg_sb", [128, 32], F32)
        t1_sb = P.sb("t1_sb", [128, 1], F32)
        oh_sb = P.sb("oh_sb", [128, 32], BF16)
        tmp32 = P.sb("tmp32", [128, 32], F32)
        tmp32b = P.sb("tmp32b", [128, 32], F32)
        cnt_sb = P.sb("cnt_sb", [1, 32], F32)
        tp_ps = [P.ps(f"tp_ps{i}", [128, 128], BF16) for i in range(2)]
        lg_ps = P.ps("lg_ps", [128, 512], F32)
        ct_ps = P.ps("ct_ps", [128, 512], F32)
        ld = P.sem("ld")
        gld = P.sem("gld")
        t2v = P.sem("t2v")
        ss = P.sem("ss")
        vv = P.sem("vv")
        st = P.sem("st")

        for name, sb_t in (("c_ident_bf", ident_bf), ("c_ones_col", ones_col),
                           ("c_iota32", iota32)):
            P.emit("gpsimd", lambda e, s=sb_t, n=name: e.dma_start(out=s[:], in_=cin[n]),
                   inc=(gld, 16))
        P.emit("gpsimd", lambda e: e.dma_start(
            out=wr_sb.rearrange("p (kc e) -> p kc e", kc=NFC),
            in_=wr.rearrange("(kc p) e -> p kc e", p=128)), inc=(gld, 16))
        LDC = gld.n
        P.emit("vector", lambda e: e.memset(lns["eps"][:], 1e-5), inc=(vv, 1))
        tp_free = [0, 0]
        lg_free = (vv, 0)
        P._lnf = []
        P._oh_guard = 0
        for tt in range(NTT):
            xb = xtile[tt % 2]
            w_pre = [(vv, P._lnf[tt - 2])] if tt >= 2 else [(gld, LDC)]
            ldv = P.emit("sync",
                         lambda e, xb=xb, row0=tt * 128:
                         e.dma_start(out=xb[:], in_=xin[row0:row0 + 128, :]),
                         wait=w_pre, inc=(ld, 16))
            v4 = emit_ln(P, (ss, vv), xb[:], lns, xhat_tm[:],
                         pre=[(ld, ldv), (vv, P._lnf[tt - 1] if tt >= 1 else 1)])
            evs = []
            for fc in range(NFC):
                par = fc % 2
                tv = P.emit("tensor",
                            lambda e, pt=tp_ps[par], fc=fc:
                            e.transpose(pt[:, 0:128],
                                        xhat_tm[:, fc * 128:(fc + 1) * 128],
                                        ident_bf[:]),
                            wait=[(vv, v4), (vv, tp_free[par])], inc=(t2v, 1))
                ev = P.emit("vector",
                            lambda e, pt=tp_ps[par], fc=fc:
                            e.tensor_copy(xhat_fm[:, fc * 128:(fc + 1) * 128],
                                          pt[:, 0:128]),
                            wait=[(t2v, tv)], inc=(vv, 1))
                tp_free[par] = ev
                evs.append(ev)
            P._lnf.append(evs[-1])
            pairs = [(xhat_fm[:, kc * 128:(kc + 1) * 128],
                      wr_sb[:, kc * 32:(kc + 1) * 32]) for kc in range(NFC)]
            tvl = P.mm_group(lg_ps[:, 0:32], pairs,
                             wait=[(vv, evs[-1]), lg_free],
                             inc=(t2v, 1), skip_group_check=True)
            evl = P.emit("scalar",
                         lambda e: e.activation(lg_sb[:], lg_ps[:, 0:32], AF.Copy),
                         wait=[(t2v, tvl)], inc=(ss, 1))
            lg_free = (ss, evl)
            g1 = P.emit("vector", lambda e: e.reduce_max(t1_sb[:], lg_sb[:], axis=AX.X),
                        wait=[(ss, evl)], inc=(vv, 1))
            g2 = P.emit("vector", lambda e: e.tensor_scalar(tmp32[:], lg_sb[:],
                                                            t1_sb[:], None, ALU.is_equal),
                        wait=[(vv, g1)], inc=(vv, 1))
            g3 = P.emit("vector", lambda e: e.tensor_tensor(tmp32b[:], iota32[:],
                                                            tmp32[:], ALU.mult),
                        wait=[(vv, g2)], inc=(vv, 1))
            g4 = P.emit("vector", lambda e: e.scalar_tensor_tensor(
                tmp32b[:], tmp32[:], -32768.0, tmp32b[:], ALU.mult, ALU.add),
                wait=[(vv, g3)], inc=(vv, 1))
            g5 = P.emit("vector", lambda e: e.tensor_reduce(t1_sb[:], tmp32b[:],
                                                            axis=AX.X, op=ALU.min),
                        wait=[(vv, g4)], inc=(vv, 1))
            g6 = P.emit("vector", lambda e: e.tensor_scalar(t1_sb[:], t1_sb[:],
                                                            32768.0, None, ALU.add),
                        wait=[(vv, g5)], inc=(vv, 1))
            g7 = P.emit("vector", lambda e: e.tensor_scalar(oh_sb[:], iota32[:],
                                                            t1_sb[:], None, ALU.is_equal),
                        wait=[(vv, g6), (t2v, P._oh_guard)], inc=(vv, 1))
            tvc = P.emit("tensor",
                         lambda e, st_=(tt == 0), sp_=(tt == NTT - 1): e.matmul(
                             ct_ps[0:1, 0:32], ones_col[:], oh_sb[:],
                             start=st_, stop=sp_, skip_group_check=True),
                         wait=[(vv, g7)], inc=(t2v, 1))
            P._oh_guard = tvc
        evc = P.emit("vector", lambda e: e.tensor_copy(cnt_sb[:], ct_ps[0:1, 0:32]),
                     wait=[(t2v, t2v.n)], inc=(vv, 1))
        P.emit("sync", lambda e: e.dma_start(out=counts, in_=cnt_sb[:]),
               wait=[(vv, evc)], inc=(st, 16))
        P.raw("sync", lambda e, n=st.n: e.wait_ge(st.h, n))
        P.finish()
    return nc




MOE_DEBUG_NTILE = None
MOE_DEBUG_SKIP = ("gxclamp",)
MOE_DEBUG_DUMP = False


def build_l1_moe():
    nc = bass.Bass("TRN2", target_bir_lowering=False, debug=False, num_devices=8)
    xin = nc.dram_tensor("xin", [TOK, DM], F32, kind="ExternalInput").ap()
    wr = nc.dram_tensor("wr", [DM, 32], F32, kind="ExternalInput").ap()
    wg = nc.dram_tensor("wg", [32, DM, DM], F32, kind="ExternalInput").ap()
    wu = nc.dram_tensor("wu", [32, DM, DM], F32, kind="ExternalInput").ap()
    wd = nc.dram_tensor("wd", [32, DM, DM], F32, kind="ExternalInput").ap()
    call = nc.dram_tensor("call", [8, 32], F32, kind="ExternalInput").ap()
    cmask = nc.dram_tensor("cmask", [8, 1], F32, kind="ExternalInput").ap()
    cin = declare_consts(nc)
    xout = nc.dram_tensor("xout", [TOK, DM], F32, kind="ExternalOutput").ap()
    xhat_d = nc.dram_tensor("xhat_d", [TOK + 1, DM], BF16, kind="Internal").ap()
    y_slots = nc.dram_tensor("y_slots", [8193, DM], BF16, kind="Internal").ap()
    t2s_d = nc.dram_tensor("t2s_d", [TOK, 1], F32, kind="Internal").ap()
    smap_d = nc.dram_tensor("smap_d", [8192 + 8, 4], I32, kind="Internal").ap()
    dbg_smap = (nc.dram_tensor("dbg_smap", [8192 + 8, 4], I32, kind="ExternalOutput").ap()
                if MOE_DEBUG_DUMP else None)
    dbg_slot = (nc.dram_tensor("dbg_slot", [128, NTT], F32, kind="ExternalOutput").ap()
                if MOE_DEBUG_DUMP else None)
    dbg_t2e = (nc.dram_tensor("dbg_t2e", [1, 64], F32, kind="ExternalOutput").ap()
               if MOE_DEBUG_DUMP else None)
    dbg_y = (nc.dram_tensor("dbg_y", [TOK + 1, DM], BF16, kind="ExternalOutput").ap()
             if MOE_DEBUG_DUMP else None)
    dbg_yt = (nc.dram_tensor("dbg_yt", [TOK, DM], BF16, kind="ExternalOutput").ap()
              if MOE_DEBUG_DUMP else None)
    dbg_t2s = (nc.dram_tensor("dbg_t2s", [TOK, 1], F32, kind="ExternalOutput").ap()
               if MOE_DEBUG_DUMP else None)
    dbg_gate = (nc.dram_tensor("dbg_gate", [128, NTT], F32, kind="ExternalOutput").ap()
                if MOE_DEBUG_DUMP else None)
    wg2 = wg.rearrange("e (kc p) f -> (e kc p) f", p=128)
    wu2 = wu.rearrange("e (kc p) f -> (e kc p) f", p=128)
    wd2 = wd.rearrange("e (kc p) f -> (e kc p) f", p=128)
    NTILE = 64

    with ExitStack() as ctx:
        P = PB(nc, ctx)
        ident_bf = P.sb("ident_bf", [128, 128], BF16)
        ident_f32 = P.sb("ident_f32", [128, 128], F32)
        ones_row = P.sb("ones_row", [1, 128], BF16)
        ones_rowf = P.sb("ones_rowf", [1, 128], F32)
        ones_col = P.sb("ones_col", [128, 1], BF16)
        iota32 = P.sb("iota32", [128, 32], F32)
        tri_cum = P.sb("tri_cum", [128, 128], BF16)
        tri32s = P.sb("tri32s", [32, 32], BF16)
        thr32 = P.sb("thr32", [32, 32], F32)
        k64 = P.sb("k64", [64, 1], F32)
        iotatok = P.sb("iotatok", [128, 1], F32)
        wr_sb = P.sb("wr_sb", [128, NFC * 32], BF16)
        call_sb = P.sb("call_sb", [8, 32], F32)
        cmask_sb = P.sb("cmask_sb", [8, 1], F32)
        xtile = [P.sb(f"xtile{i}", [128, DM], F32) for i in range(2)]
        lns = ln_scratch(P, "m")
        xhat_tm = P.sb("xhat_tm", [128, DM], BF16)
        lg_sb = P.sb("lg_sb", [128, 32], F32)
        t1_sb = P.sb("t1_sb", [128, 1], F32)
        nmr_sb = P.sb("nmr_sb", [128, 1], F32)
        gate_all = P.sb("gate_all", [128, NTT], F32)
        oh_all = P.sb("oh_all", [128, NTT * 32], BF16)
        tmp32 = P.sb("tmp32", [128, 32], F32)
        tmp32b = P.sb("tmp32b", [128, 32], F32)
        gp_all = P.sb("gp_all", [128, NTT * 32], F32)
        slot_sb = P.sb("slot_sb", [128, 1], F32)
        acc_sb = P.sb("acc_sb", [128, 1], F32)
        slot_i = P.sb("slot_i", [128, 1], I32)
        pay_f = P.sb("pay_f", [128, 1], F32)
        pay_i = P.sb("pay_i", [128, 4], I32)
        big_i = P.sb("big_i", [128, 4], I32)
        carry = P.sb("carry", [1, 32], F32)
        base_row = P.sb("base_row", [1, 32], F32)
        basebc = P.sb("basebc", [128, 32], F32)
        nacc_c = P.sb("nacc_c", [32, 1], F32)
        ntile_c = P.sb("ntile_c", [32, 1], BF16)
        segadj_r = P.sb("segadj_r", [1, 32], F32)
        segadjbc = P.sb("segadjbc", [128, 32], F32)
        seg_r = P.sb("seg_r", [1, 32], F32)
        base_c = P.sb("base_c", [32, 1], F32)
        cnt_c = P.sb("cnt_c", [32, 1], F32)
        segtile_c = P.sb("segtile_c", [32, 1], F32)
        segbc64 = P.sb("segbc64", [64, 32], F32)
        t2e_c = P.sb("t2e_c", [64, 1], F32)
        t2e_r = P.sb("t2e_r", [1, 64], F32)
        eidx_sb = P.sb("eidx_sb", [128, 1], F32)
        wbase_f = P.sb("wbase_f", [128, 1], F32)
        idx8_f = P.sb("idx8_f", [128, 8], F32)
        idx8_i = P.sb("idx8_i", [128, 8], I32)
        cmp32 = P.sb("cmp32", [32, 32], F32)
        tmp1_32 = P.sb("tmp1_32", [32, 1], F32)
        cmp64 = P.sb("cmp64", [64, 32], F32)
        tokmap = [P.sb(f"tokmap{i}", [128, 4], I32) for i in range(2)]
        zero512 = P.sb("zero512", [128, 1024], BF16)
        xg_tm = [P.sb(f"xg_tm{i}", [128, DM], BF16) for i in range(2)]
        xg_fm = P.sb("xg_fm", [128, DM], BF16)
        gsil = P.sb("gsil", [128, 128], BF16)
        hh_sb = P.sb("hh_sb", [128, DM], BF16)
        y_sb = [P.sb(f"y_sb{i}", [128, DM], BF16) for i in range(2)]
        wgslab = [P.sb(f"wgslab{i}", [128, NFC * DM], BF16) for i in range(2)]
        wuslab = [P.sb(f"wuslab{i}", [128, NFC * DM], BF16) for i in range(2)]
        wdslab = [P.sb(f"wdslab{i}", [128, NFC * DM], BF16) for i in range(2)]
        yt_sb = [P.sb(f"yt_sb{i}", [128, DM], BF16) for i in range(2)]
        xh_sb = P.sb("xh_sb", [128, DM], BF16)
        tp_ps = [P.ps(f"tp_ps{i}", [128, 128], BF16) for i in range(2)]
        lg_ps = P.ps("lg_ps", [128, 512], F32)
        sm_ps = P.ps("sm_ps", [128, 512], F32)
        gu_ps = [P.ps(f"gu_ps{i}", [128, 512], F32) for i in range(2)]
        y_ps = [P.ps(f"y_ps{i}", [128, 512], F32) for i in range(2)]

        ld = P.sem("ld")
        gld = P.sem("gld")
        t2v = P.sem("t2v")
        ss = P.sem("ss")
        vv = P.sem("vv")
        st = P.sem("st")
        gp = P.sem("gp")

        for name, sb_t in (("c_ident_bf", ident_bf), ("c_ident_f32", ident_f32),
                           ("c_ones_row", ones_row), ("c_ones_col", ones_col),
                           ("c_iota32", iota32), ("c_tri_cum", tri_cum),
                           ("c_tri32s", tri32s), ("c_thr32", thr32),
                           ("c_k64", k64), ("c_iotatok", iotatok)):
            P.emit("gpsimd", lambda e, s=sb_t, n=name: e.dma_start(out=s[:], in_=cin[n]),
                   inc=(gld, 16))
        P.emit("gpsimd", lambda e: e.dma_start(out=ones_rowf[:], in_=cin["c_ones_row"]),
               inc=(gld, 16))
        P.emit("gpsimd", lambda e: e.dma_start(
            out=wr_sb.rearrange("p (kc e) -> p kc e", kc=NFC),
            in_=wr.rearrange("(kc p) e -> p kc e", p=128)), inc=(gld, 16))
        P.emit("gpsimd", lambda e: e.dma_start(out=call_sb[:], in_=call), inc=(gld, 16))
        P.emit("gpsimd", lambda e: e.dma_start(out=cmask_sb[:], in_=cmask), inc=(gld, 16))
        LDC = gld.n
        P.emit("vector", lambda e: e.memset(lns["eps"][:], 1e-5), inc=(vv, 1))
        P._carry_v = P.emit("vector", lambda e: e.memset(carry[:], 0.0), inc=(vv, 1))
        zv = P.emit("vector", lambda e: e.memset(zero512[:], 0.0), inc=(vv, 1))
        bigv0 = P.emit("vector", lambda e: e.memset(pay_f[:], float(TOK)), inc=(vv, 1))
        bigv1 = P.emit("vector", lambda e: e.tensor_copy(big_i[:, 0:1], pay_f[:]),
                       wait=[(vv, bigv0)], inc=(vv, 1))
        bigv2 = P.emit("vector", lambda e: e.tensor_copy(big_i[:, 1:2], pay_f[:]),
                       wait=[(vv, bigv1)], inc=(vv, 1))
        bigv3 = P.emit("vector", lambda e: e.tensor_copy(big_i[:, 2:3], pay_f[:]),
                       wait=[(vv, bigv2)], inc=(vv, 1))
        bigv = P.emit("vector", lambda e: e.tensor_copy(big_i[:, 3:4], pay_f[:]),
                      wait=[(vv, bigv3)], inc=(vv, 1))
        pz = P.emit("vector", lambda e: e.memset(pay_i[:], 0), inc=(vv, 1))
        P.emit("sync",
               lambda e: e.dma_start(out=y_slots[8192:8193, :],
                                     in_=zero512[0:1, :]),
               wait=[(vv, zv)], inc=(ld, 16))
        for i in range(64):
            P.emit("sync",
                   lambda e, i=i: e.dma_start(
                       out=smap_d[i * 128:(i + 1) * 128, :], in_=big_i[:]),
                   wait=[(vv, bigv)], inc=(ld, 16))
        INIT_LD = ld.n

        tvb = P.emit("tensor",
                     lambda e: e.matmul(sm_ps[0:1, 0:32], cmask_sb[:], call_sb[:],
                                        start=True, stop=True, skip_group_check=True),
                     wait=[(gld, LDC)], inc=(t2v, 1))
        evb = P.emit("vector", lambda e: e.tensor_copy(base_row[:], sm_ps[0:1, 0:32]),
                     wait=[(t2v, tvb)], inc=(vv, 1))
        tvbb = P.emit("tensor",
                      lambda e: e.matmul(sm_ps[:, 32:64], ones_rowf[:], base_row[:],
                                         start=True, stop=True, skip_group_check=True),
                      wait=[(vv, evb)], inc=(t2v, 1))
        evbb = P.emit("vector", lambda e: e.tensor_copy(basebc[:], sm_ps[:, 32:64]),
                      wait=[(t2v, tvbb)], inc=(vv, 1))

        tp_free = [0, 0]
        lgf = (vv, 0)
        P._lnf = []
        xhg = nc.sbuf_tensor("xhat_fm_r", [128, NFC * 128], BF16, side="right")
        xhat_fm = xhg.__enter__()
        for tt in range(NTT):
            xb = xtile[tt % 2]
            ohs = oh_all[:, tt * 32:(tt + 1) * 32]
            w_pre = [(vv, P._lnf[tt - 2])] if tt >= 2 else [(gld, LDC)]
            ldv = P.emit("sync",
                         lambda e, xb=xb, row0=tt * 128:
                         e.dma_start(out=xb[:], in_=xin[row0:row0 + 128, :]),
                         wait=w_pre, inc=(ld, 16))
            v4 = emit_ln(P, (ss, vv), xb[:], lns, xhat_tm[:],
                         pre=[(ld, ldv), (vv, P._lnf[tt - 1] if tt >= 1 else 1)])
            dxh = P.emit("sync",
                         lambda e, tt=tt: e.dma_start(
                             out=xhat_d[tt * 128:(tt + 1) * 128, :], in_=xhat_tm[:]),
                         wait=[(vv, v4)], inc=(ld, 16))
            evs = []
            for fc in range(NFC):
                par = fc % 2
                tv = P.emit("tensor",
                            lambda e, pt=tp_ps[par], fc=fc:
                            e.transpose(pt[:, 0:128],
                                        xhat_tm[:, fc * 128:(fc + 1) * 128],
                                        ident_bf[:]),
                            wait=[(vv, v4), (vv, tp_free[par])], inc=(t2v, 1))
                ev = P.emit("vector",
                            lambda e, pt=tp_ps[par], fc=fc:
                            e.tensor_copy(xhat_fm[:, fc * 128:(fc + 1) * 128],
                                          pt[:, 0:128]),
                            wait=[(t2v, tv)], inc=(vv, 1))
                tp_free[par] = ev
                evs.append(ev)
            P._lnf.append(evs[-1])
            pairs = [(xhat_fm[:, kc * 128:(kc + 1) * 128],
                      wr_sb[:, kc * 32:(kc + 1) * 32]) for kc in range(NFC)]
            tvl = P.mm_group(lg_ps[:, 0:32], pairs,
                             wait=[(vv, evs[-1]), lgf, (ld, dxh)],
                             inc=(t2v, 1), skip_group_check=True)
            evl = P.emit("scalar",
                         lambda e: e.activation(lg_sb[:], lg_ps[:, 0:32], AF.Copy),
                         wait=[(t2v, tvl)], inc=(ss, 1))
            lgf = (ss, evl)
            g1 = P.emit("vector", lambda e: e.reduce_max(t1_sb[:], lg_sb[:], axis=AX.X,
                                                         negate=True),
                        wait=[(ss, evl)], inc=(vv, 1))
            sgx = P.emit("scalar",
                         lambda e: e.activation(tmp32[:], lg_sb[:], AF.Exp,
                                                bias=t1_sb[:], accum_out=nmr_sb[:]),
                         wait=[(vv, g1)], inc=(ss, 1))
            gt = P.emit("vector",
                        lambda e, tt=tt: e.reciprocal(gate_all[:, tt:tt + 1],
                                                      nmr_sb[:]),
                        wait=[(ss, sgx)], inc=(vv, 1))
            g1b = P.emit("vector", lambda e: e.tensor_scalar(t1_sb[:], t1_sb[:], -1.0,
                                                             None, ALU.mult),
                         wait=[(vv, gt)], inc=(vv, 1))
            g2 = P.emit("vector", lambda e: e.tensor_scalar(tmp32[:], lg_sb[:],
                                                            t1_sb[:], None, ALU.is_equal),
                        wait=[(vv, g1b)], inc=(vv, 1))
            g3 = P.emit("vector", lambda e: e.tensor_tensor(tmp32b[:], iota32[:],
                                                            tmp32[:], ALU.mult),
                        wait=[(vv, g2)], inc=(vv, 1))
            g4 = P.emit("vector", lambda e: e.scalar_tensor_tensor(
                tmp32b[:], tmp32[:], -32768.0, tmp32b[:], ALU.mult, ALU.add),
                wait=[(vv, g3)], inc=(vv, 1))
            g5 = P.emit("vector", lambda e: e.tensor_reduce(t1_sb[:], tmp32b[:],
                                                            axis=AX.X, op=ALU.min),
                        wait=[(vv, g4)], inc=(vv, 1))
            g6 = P.emit("vector", lambda e: e.tensor_scalar(t1_sb[:], t1_sb[:],
                                                            32768.0, None, ALU.add),
                        wait=[(vv, g5)], inc=(vv, 1))
            g7 = P.emit("vector", lambda e, ohs=ohs: e.tensor_scalar(
                ohs, iota32[:], t1_sb[:], None, ALU.is_equal),
                wait=[(vv, g6)], inc=(vv, 1))
            P.mm_group(sm_ps[:, 64:96], [(tri_cum[:], ohs)],
                       wait=[(vv, g7), (vv, P._carry_v)], first_start=True,
                       last_stop=False, skip_group_check=True)
            P.emit("tensor",
                   lambda e: e.matmul(sm_ps[:, 64:96], ones_rowf[:], carry[:],
                                      start=False, stop=False, skip_group_check=True),
                   wait=[(vv, P._carry_v)])
            tvp3 = P.emit("tensor",
                          lambda e: e.matmul(sm_ps[:, 64:96], ones_rowf[:], base_row[:],
                                             start=False, stop=True,
                                             skip_group_check=True),
                          wait=[(vv, evb)], inc=(t2v, 1))
            evp = P.emit("vector",
                         lambda e, tt=tt: e.tensor_copy(
                             gp_all[:, tt * 32:(tt + 1) * 32], sm_ps[:, 64:96]),
                         wait=[(t2v, tvp3)], inc=(vv, 1))
            tvcs = P.emit("tensor",
                          lambda e, ohs=ohs: e.matmul(sm_ps[0:1, 96:128], ones_col[:],
                                                      ohs, start=True, stop=True,
                                                      skip_group_check=True),
                          wait=[(vv, evp)], inc=(t2v, 1))
            cup = P.emit("vector",
                         lambda e: e.tensor_tensor(carry[:], carry[:],
                                                   sm_ps[0:1, 96:128], ALU.add),
                         wait=[(t2v, tvcs)], inc=(vv, 1))
            P._carry_v = cup
        ROUTE_VV = vv.n
        xhg.__exit__(None, None, None)

        def small_transpose(in_ap, out_sb, K, M, psr, waitv, f32=True):
            idn = ident_f32[0:K, 0:K]
            tvx = P.emit("tensor",
                         lambda e: e.transpose(psr, in_ap, idn),
                         wait=waitv, inc=(t2v, 1))
            evx = P.emit("vector", lambda e: e.tensor_copy(out_sb, psr),
                         wait=[(t2v, tvx)], inc=(vv, 1))
            return evx

        if "book" in MOE_DEBUG_SKIP:
            e5 = evaj = 0
        e1 = small_transpose(base_row[:], base_c[:], 1, 32, sm_ps[0:32, 128:129],
                             [(vv, ROUTE_VV)]) if "book" not in MOE_DEBUG_SKIP else 0
        if "book" not in MOE_DEBUG_SKIP:
            e2 = small_transpose(carry[:], cnt_c[:], 1, 32, sm_ps[0:32, 129:130],
                                 [(vv, max(P._carry_v, e1))])
            s1_ = P.emit("vector", lambda e: e.tensor_scalar(nacc_c[:], base_c[:], -1.0,
                                                             float(CAP), ALU.mult, ALU.add),
                         wait=[(vv, max(e1, e2))], inc=(vv, 1))
            s2_ = P.emit("vector", lambda e: e.tensor_single_scalar(nacc_c[:], nacc_c[:],
                                                                    0.0, ALU.max),
                         wait=[(vv, s1_)], inc=(vv, 1))
            s3_ = P.emit("vector", lambda e: e.tensor_tensor(nacc_c[:], nacc_c[:], cnt_c[:],
                                                             ALU.min),
                         wait=[(vv, s2_)], inc=(vv, 1))
            s4_ = P.emit("vector", lambda e: e.tensor_scalar(cmp32[:], thr32[:], nacc_c[:],
                                                             None, ALU.is_lt),
                         wait=[(vv, s3_)], inc=(vv, 1))
            s5_ = P.emit("vector", lambda e: e.reduce_sum(tmp1_32[:], cmp32[:], axis=AX.X),
                         wait=[(vv, s4_)], inc=(vv, 1))
            s6_ = P.emit("vector", lambda e: e.tensor_copy(ntile_c[:], tmp1_32[:]),
                         wait=[(vv, s5_)], inc=(vv, 1))
            tvsg = P.emit("tensor",
                          lambda e: e.matmul(sm_ps[0:32, 130:131], tri32s[:], ntile_c[:],
                                             start=True, stop=True, skip_group_check=True),
                          wait=[(vv, s6_)], inc=(t2v, 1))
            evsg = P.emit("vector", lambda e: e.tensor_copy(segtile_c[:],
                                                            sm_ps[0:32, 130:131]),
                          wait=[(t2v, tvsg)], inc=(vv, 1))
            e3 = small_transpose(segtile_c[:], seg_r[:], 32, 1, sm_ps[0:1, 131:163],
                                 [(vv, evsg)])
            a1_ = P.emit("vector", lambda e: e.tensor_scalar(segadj_r[:], seg_r[:], 128.0,
                                                             None, ALU.mult),
                         wait=[(vv, e3)], inc=(vv, 1))
            a2_ = P.emit("vector", lambda e: e.tensor_tensor(segadj_r[:], segadj_r[:],
                                                             base_row[:], ALU.subtract),
                         wait=[(vv, a1_)], inc=(vv, 1))
            a3_ = P.emit("vector", lambda e: e.tensor_scalar(segadj_r[:], segadj_r[:],
                                                             -1.0, None, ALU.add),
                         wait=[(vv, a2_)], inc=(vv, 1))
            tvaj = P.emit("tensor",
                          lambda e: e.matmul(sm_ps[:, 163:195], ones_rowf[:], segadj_r[:],
                                             start=True, stop=True, skip_group_check=True),
                          wait=[(vv, a3_)], inc=(t2v, 1))
            evaj = P.emit("vector", lambda e: e.tensor_copy(segadjbc[:], sm_ps[:, 163:195]),
                          wait=[(t2v, tvaj)], inc=(vv, 1))
            tvsb = P.emit("tensor",
                          lambda e: e.matmul(sm_ps[0:64, 195:227], ones_rowf[0:1, 0:64],
                                             seg_r[:], start=True, stop=True,
                                             skip_group_check=True),
                          wait=[(vv, max(e3, evaj))], inc=(t2v, 1))
            evsb = P.emit("vector", lambda e: e.tensor_copy(segbc64[:],
                                                            sm_ps[0:64, 195:227]),
                          wait=[(t2v, tvsb)], inc=(vv, 1))
            b1_ = P.emit("vector", lambda e: e.tensor_scalar(cmp64[:], segbc64[:],
                                                             k64[0:64, :], None, ALU.is_le),
                         wait=[(vv, evsb)], inc=(vv, 1))
            b2_ = P.emit("vector", lambda e: e.reduce_sum(t2e_c[:], cmp64[:], axis=AX.X),
                         wait=[(vv, b1_)], inc=(vv, 1))
            b3_ = P.emit("vector", lambda e: e.tensor_scalar(t2e_c[:], t2e_c[:], -1.0,
                                                             None, ALU.add),
                         wait=[(vv, b2_)], inc=(vv, 1))
            b4_ = P.emit("vector", lambda e: e.tensor_single_scalar(t2e_c[:], t2e_c[:],
                                                                    0.0, ALU.max),
                         wait=[(vv, b3_)], inc=(vv, 1))
            e4 = small_transpose(t2e_c[:], t2e_r[:], 64, 1, sm_ps[0:1, 227:291],
                                 [(vv, b4_)])
            e5 = e4

        scat_g = 0
        for tt in (range(NTT) if "slot" not in MOE_DEBUG_SKIP else []):
            gps = gp_all[:, tt * 32:(tt + 1) * 32]
            ohs = oh_all[:, tt * 32:(tt + 1) * 32]
            c1 = P.emit("vector", lambda e, gps=gps: e.tensor_single_scalar(
                tmp32[:], gps, CAP + 0.5, ALU.is_lt),
                wait=[(vv, max(ROUTE_VV, evaj)), (gp, scat_g),
                      (ld, getattr(P, "_t2s_guard", 0))], inc=(vv, 1))
            c2 = P.emit("vector", lambda e, ohs=ohs: e.tensor_tensor(
                tmp32[:], tmp32[:], ohs, ALU.mult),
                wait=[(vv, c1)], inc=(vv, 1))
            c3 = P.emit("vector", lambda e, gps=gps: e.tensor_tensor(
                tmp32b[:], gps, segadjbc[:], ALU.add),
                wait=[(vv, c2)], inc=(vv, 1))
            c4 = P.emit("vector", lambda e: e.tensor_tensor(
                tmp32b[:], tmp32b[:], tmp32[:], ALU.mult),
                wait=[(vv, c3)], inc=(vv, 1))
            c5 = P.emit("vector", lambda e: e.reduce_sum(slot_sb[:], tmp32b[:],
                                                         axis=AX.X),
                        wait=[(vv, c4)], inc=(vv, 1))
            c6 = P.emit("vector", lambda e: e.reduce_sum(acc_sb[:], tmp32[:],
                                                         axis=AX.X),
                        wait=[(vv, c5)], inc=(vv, 1))
            c7 = P.emit("vector", lambda e: e.scalar_tensor_tensor(
                slot_sb[:], acc_sb[:], -8192.0, slot_sb[:], ALU.mult, ALU.add),
                wait=[(vv, c6)], inc=(vv, 1))
            c8 = P.emit("vector", lambda e: e.tensor_scalar(slot_sb[:], slot_sb[:],
                                                            8192.0, None, ALU.add),
                        wait=[(vv, c7)], inc=(vv, 1))
            c8b = P.emit("vector", lambda e: e.tensor_single_scalar(
                slot_sb[:], slot_sb[:], 8192.0, ALU.min),
                wait=[(vv, c8)], inc=(vv, 1))
            c8c = P.emit("vector", lambda e: e.tensor_single_scalar(
                slot_sb[:], slot_sb[:], 0.0, ALU.max),
                wait=[(vv, c8b)], inc=(vv, 1))
            c9 = P.emit("vector", lambda e: e.tensor_copy(slot_i[:], slot_sb[:]),
                        wait=[(vv, c8c)], inc=(vv, 1))
            ts2s = P.emit("sync",
                          lambda e, tt=tt: e.dma_start(
                              out=t2s_d[tt * 128:(tt + 1) * 128, :], in_=slot_sb[:]),
                          wait=[(vv, c9)], inc=(ld, 16))
            P._t2s_guard = ts2s
            if MOE_DEBUG_DUMP:
                P.emit("sync", lambda e, tt=tt: e.dma_start(
                    out=dbg_slot[:, tt:tt + 1], in_=slot_sb[:],
                    allow_slow_non_contiguous=True),
                    wait=[(vv, c9)], inc=(st, 16))
            p1 = P.emit("vector", lambda e, tt=tt: e.tensor_scalar(
                pay_f[:], iotatok[:], float(tt * 128), None, ALU.add),
                wait=[(vv, c9)], inc=(vv, 1))
            p2 = P.emit("vector", lambda e: e.tensor_copy(pay_i[:, 0:1], pay_f[:]),
                        wait=[(vv, p1), (vv, pz)], inc=(vv, 1))
            if "scat" not in MOE_DEBUG_SKIP:
                scat_g = P.emit("gpsimd",
                                lambda e: e.indirect_dma_start(
                                    smap_d[:, :],
                                    IndirectOffsetOnAxis(slot_i[:], 0),
                                    pay_i[:], None),
                                wait=[(vv, p2), (ld, INIT_LD)], inc=(gp, 16))
        fl = gp.n
        for i in range(16):
            fl = P.emit("gpsimd",
                        lambda e, i=i: e.dma_start(
                            out=smap_d[8192 + 4:8192 + 8, :] if i % 2 == 0
                            else pay_i[0:4, :],
                            in_=pay_i[0:4, :] if i % 2 == 0
                            else smap_d[8192 + 4:8192 + 8, :]),
                        wait=[(gp, fl)], inc=(gp, 16))
        SCAT_GP = gp.n
        if MOE_DEBUG_DUMP:
            for i in range(65):
                P.emit("sync", lambda e, i=i: e.dma_start(
                    out=dbg_smap[i * 128:min((i + 1) * 128, 8200), :],
                    in_=smap_d[i * 128:min((i + 1) * 128, 8200), :]),
                    wait=[(gp, SCAT_GP)], inc=(st, 16))

        NTILE_RUN = MOE_DEBUG_NTILE if MOE_DEBUG_NTILE is not None else NTILE
        wfree = [(vv, 0), (vv, 0)]
        wgath_prev = [0, 0]
        ysb_free = [0, 0]
        tkfree = [(vv, 0), (vv, 0)]
        xgfree = [(vv, 0), (vv, 0)]
        yfree = [(vv, 0), (vv, 0)]
        gufree = [(vv, 0), (vv, 0)]
        smfree = (vv, 0)
        for k in range(NTILE_RUN):
            par = k % 2
            tkm = tokmap[par]
            ldt = P.emit("gpsimd",
                         lambda e, k=k, tkm=tkm: e.dma_start(
                             out=tkm[:], in_=smap_d[k * 128:(k + 1) * 128, :]),
                         wait=[(gp, SCAT_GP), tkfree[par]], inc=(gp, 16))
            xg = xg_tm[par]
            if "xgath" not in MOE_DEBUG_SKIP:
                if "gxclamp" in MOE_DEBUG_SKIP:
                    f1 = P.emit("vector", lambda e, tkm=tkm: e.tensor_copy(
                        pay_f[:], tkm[:, 0:1]), wait=[(gp, ldt)], inc=(vv, 1))
                    f2 = P.emit("vector", lambda e: e.tensor_single_scalar(
                        pay_f[:], pay_f[:], float(TOK), ALU.min),
                        wait=[(vv, f1)], inc=(vv, 1))
                    f3 = P.emit("vector", lambda e: e.tensor_single_scalar(
                        pay_f[:], pay_f[:], 0.0, ALU.max),
                        wait=[(vv, f2)], inc=(vv, 1))
                    f4 = P.emit("vector", lambda e, tkm=tkm: e.tensor_copy(
                        tkm[:, 0:1], pay_f[:]), wait=[(vv, f3)], inc=(vv, 1))
                    P._gxw = f4
                else:
                    P._gxw = 0
                idx_ap = big_i[:, 0:1] if "gxconst" in MOE_DEBUG_SKIP else tkm[:, 0:1]
                gx = P.emit("gpsimd",
                            lambda e, xg=xg, idx_ap=idx_ap: e.indirect_dma_start(
                                xg[:], None, xhat_d[:, :],
                                IndirectOffsetOnAxis(idx_ap, 0)),
                            wait=[(gp, ldt), xgfree[par], (vv, P._gxw)],
                            inc=(gp, 16))
            GXv = gp.n
            tve = P.emit("tensor",
                         lambda e, k=k: e.matmul(sm_ps[:, 300:301], ones_rowf[:],
                                                 t2e_r[0:1, k:k + 1], start=True,
                                                 stop=True, skip_group_check=True),
                         wait=[(vv, e5), smfree], inc=(t2v, 1))
            eve = P.emit("vector",
                         lambda e: e.tensor_copy(eidx_sb[:], sm_ps[:, 300:301]),
                         wait=[(t2v, tve)], inc=(vv, 1))
            smfree = (vv, eve)
            w1 = P.emit("vector",
                        lambda e: e.tensor_scalar(wbase_f[:], eidx_sb[:],
                                                  float(NFC * 128), None, ALU.mult),
                        wait=[(vv, eve)], inc=(vv, 1))
            w2_ = P.emit("vector",
                         lambda e: e.tensor_tensor(wbase_f[:], wbase_f[:],
                                                   iotatok[:], ALU.add),
                         wait=[(vv, w1)], inc=(vv, 1))
            w3_ = w2_
            for kc in range(NFC):
                w3_ = P.emit("vector",
                             lambda e, kc=kc: e.tensor_scalar(
                                 idx8_f[:, kc:kc + 1], wbase_f[:], float(kc * 128),
                                 None, ALU.add),
                             wait=[(vv, w2_ if kc == 0 else w3_)], inc=(vv, 1))
            wi = P.emit("vector", lambda e: e.tensor_copy(idx8_i[:], idx8_f[:]),
                        wait=[(vv, w3_), (gp, wgath_prev[par])], inc=(vv, 1))
            if "wload" not in MOE_DEBUG_SKIP:
                for (slab, wsrc) in ((wgslab[par], wg2), (wuslab[par], wu2),
                                     (wdslab[par], wd2)):
                    for kc in range(NFC):
                        P.emit("gpsimd",
                               lambda e, slab=slab, wsrc=wsrc, kc=kc:
                               e.indirect_dma_start(
                                   slab[:, kc * DM:(kc + 1) * DM], None, wsrc,
                                   IndirectOffsetOnAxis(idx8_i[:, kc:kc + 1], 0)),
                               wait=[(vv, wi), wfree[par]] if kc == 0 else (),
                               inc=(gp, 16))
            WL = gp.n
            wgath_prev[par] = WL
            xevs = [0]
            for fc in (range(NFC) if "xtp" not in MOE_DEBUG_SKIP else []):
                tpar = fc % 2
                tv = P.emit("tensor",
                            lambda e, pt=tp_ps[tpar], fc=fc, xg=xg:
                            e.transpose(pt[:, 0:128], xg[:, fc * 128:(fc + 1) * 128],
                                        ident_bf[:]),
                            wait=[(gp, GXv), (vv, tp_free[tpar])], inc=(t2v, 1))
                ev = P.emit("vector",
                            lambda e, pt=tp_ps[tpar], fc=fc:
                            e.tensor_copy(xg_fm[:, fc * 128:(fc + 1) * 128],
                                          pt[:, 0:128]),
                            wait=[(t2v, tv)], inc=(vv, 1))
                tp_free[tpar] = ev
                xevs.append(ev)
            XG_VV = xevs[-1]
            hh_evs = [0]
            for fc in (range(NFC) if "gumm" not in MOE_DEBUG_SKIP else []):
                pg = gu_ps[0]
                pu = gu_ps[1]
                pairsg = [(wgslab[par][:, kc * DM + fc * 128: kc * DM + fc * 128 + 128],
                           xg_fm[:, kc * 128:(kc + 1) * 128]) for kc in range(NFC)]
                pairsu = [(wuslab[par][:, kc * DM + fc * 128: kc * DM + fc * 128 + 128],
                           xg_fm[:, kc * 128:(kc + 1) * 128]) for kc in range(NFC)]
                P.mm_group(pg[:, 0:128], pairsg,
                           wait=[(gp, WL), (vv, XG_VV), gufree[0]],
                           inc=(t2v, 1), skip_group_check=True)
                tg_ = t2v.n
                sgl = P.emit("scalar",
                             lambda e, pg=pg: e.activation(gsil[:], pg[:, 0:128],
                                                           AF.Silu),
                             wait=[(t2v, tg_)], inc=(ss, 1))
                gufree[0] = (ss, sgl)
                P.mm_group(pu[:, 0:128], pairsu,
                           wait=[(gp, WL), (vv, XG_VV), gufree[1]],
                           inc=(t2v, 1), skip_group_check=True)
                tu_ = t2v.n
                hm = P.emit("vector",
                            lambda e, pu=pu, fc=fc: e.tensor_tensor(
                                hh_sb[:, fc * 128:(fc + 1) * 128], pu[:, 0:128],
                                gsil[:], ALU.mult),
                            wait=[(t2v, tu_), (ss, sgl)], inc=(vv, 1))
                gufree[1] = (vv, hm)
                hh_evs.append(hm)
            HH_VV = hh_evs[-1]
            ysb = y_sb[par]
            for hg in (range(2) if "dnmm" not in MOE_DEBUG_SKIP else []):
                py = y_ps[hg]
                pairs = [(hh_sb[:, fcK * 128:(fcK + 1) * 128],
                          wdslab[par][:, fcK * DM + hg * 512: fcK * DM + hg * 512 + 512])
                         for fcK in range(NFC)]
                P.mm_group(py[:], pairs,
                           wait=[(vv, HH_VV), yfree[hg]],
                           inc=(t2v, 1), skip_group_check=True)
                ty_ = t2v.n
                ye = P.emit("scalar",
                            lambda e, py=py, hg=hg, ysb=ysb:
                            e.activation(ysb[:, hg * 512: hg * 512 + 512], py[:],
                                         AF.Copy),
                            wait=[(t2v, ty_), (ld, ysb_free[par])], inc=(ss, 1))
                yfree[hg] = (ss, ye)
            YE = ss.n
            if "yscat" not in MOE_DEBUG_SKIP:
                ysc_ld = P.emit("sync",
                                lambda e, ysb=ysb, k=k: e.dma_start(
                                    out=y_slots[k * 128:(k + 1) * 128, :], in_=ysb[:]),
                                wait=[(ss, YE)], inc=(ld, 16))
                P._ys_ld = ysc_ld
                ysc = 0
            else:
                ysc = 0
            tkfree[par] = (gp, gp.n)
            ysb_free[par] = getattr(P, "_ys_ld", 0)
            xgfree[par] = (t2v, t2v.n)
            wfree[par] = (t2v, t2v.n)
        ALL_Y = gp.n
        YS_LD = getattr(P, "_ys_ld", 0)
        if MOE_DEBUG_DUMP:
            P.emit("sync", lambda e: e.dma_start(out=dbg_t2e, in_=t2e_r[:]),
                   wait=[(vv, e5)], inc=(st, 16))
            P.emit("sync", lambda e: e.dma_start(out=dbg_t2s, in_=t2s_d),
                   wait=[(ld, getattr(P, "_t2s_guard", 0))], inc=(st, 16))
            P.emit("sync", lambda e: e.dma_start(out=dbg_gate, in_=gate_all[:]),
                   wait=[(vv, ROUTE_VV)], inc=(st, 16))
            for i in range(NTT):
                P.emit("sync", lambda e, i=i: e.dma_start(
                    out=dbg_y[i * 128:(i + 1) * 128, :],
                    in_=y_slots[i * 128:(i + 1) * 128, :]),
                    wait=[(ld, YS_LD)], inc=(st, 16))

        ep_free = [0, 0]
        ep_st = [0, 0]
        for tt in range(NTT):
            rt = xtile[tt % 2]
            yt = yt_sb[tt % 2]
            l1 = P.emit("sync",
                        lambda e, rt=rt, row0=tt * 128:
                        e.dma_start(out=rt[:], in_=xin[row0:row0 + 128, :]),
                        wait=[(vv, ep_free[tt % 2]), (st, ep_st[tt % 2])],
                        inc=(ld, 16))
            if "epi" not in MOE_DEBUG_SKIP:
                l2a = P.emit("sync",
                             lambda e, tt=tt: e.dma_start(
                                 out=pay_f[:], in_=t2s_d[tt * 128:(tt + 1) * 128, :]),
                             wait=[(vv, getattr(P, "_pf_guard", 0))], inc=(ld, 16))
                l2b = P.emit("vector",
                             lambda e: e.tensor_copy(slot_i[:], pay_f[:]),
                             wait=[(ld, l2a), (gp, getattr(P, "_l2_prev", 0))],
                             inc=(vv, 1))
                P._pf_guard = l2b
                l2 = P.emit("gpsimd",
                            lambda e, yt=yt: e.indirect_dma_start(
                                yt[:], None, y_slots[:, :],
                                IndirectOffsetOnAxis(slot_i[:], 0)),
                            wait=[(ld, YS_LD), (vv, l2b),
                                  (vv, getattr(P, "_yt_guard", 0))],
                            inc=(gp, 16))
                P._l2_prev = l2
                l3 = P.emit("sync",
                            lambda e, tt=tt: e.dma_start(
                                out=xh_sb[:], in_=xhat_d[tt * 128:(tt + 1) * 128, :]),
                            wait=[(vv, getattr(P, "_xh_guard", 0))], inc=(ld, 16))
                o1 = P.emit("vector",
                            lambda e, yt=yt, tt=tt, rt=rt: e.scalar_tensor_tensor(
                                rt[:], yt[:], gate_all[:, tt:tt + 1], rt[:],
                                ALU.mult, ALU.add),
                            wait=[(ld, l1), (gp, l2)], inc=(vv, 1))
                P._yt_guard = o1
                if MOE_DEBUG_DUMP:
                    P.emit("sync", lambda e, yt=yt, tt=tt: e.dma_start(
                        out=dbg_yt[tt * 128:(tt + 1) * 128, :], in_=yt[:]),
                        wait=[(gp, l2)], inc=(st, 16))
                    P._yt_guard = max(o1, 0)
                o2 = P.emit("vector",
                            lambda e, rt=rt: e.tensor_tensor(rt[:], rt[:], xh_sb[:],
                                                             ALU.add),
                            wait=[(vv, o1), (ld, l3)], inc=(vv, 1))
                P._xh_guard = o2
            else:
                o2 = P.emit("vector",
                            lambda e, rt=rt: e.tensor_scalar(rt[:], rt[:], 1.0,
                                                             None, ALU.mult),
                            wait=[(ld, l1)], inc=(vv, 1))
            stv = P.emit("sync",
                         lambda e, rt=rt, row0=tt * 128:
                         e.dma_start(out=xout[row0:row0 + 128, :], in_=rt[:]),
                         wait=[(vv, o2)], inc=(st, 16))
            ep_free[tt % 2] = o2
            ep_st[tt % 2] = stv
        P.raw("sync", lambda e, n=st.n: e.wait_ge(st.h, n))
        P.finish()
    return nc




def run_l1_route(inputs, x3_full):
    nc = build_l1_route()
    consts = make_consts()
    shards = _shard_x(x3_full)
    in_maps = []
    for c in range(NCORES):
        m = {"xin": shards[c],
             "wr": np.ascontiguousarray(inputs["L1_ffn_w_router"])}
        m.update(consts)
        in_maps.append(m)
    res = run_bass_kernel_spmd(nc, in_maps, core_ids=list(range(NCORES)))
    return np.stack([np.asarray(res.results[c]["counts"])[0] for c in range(NCORES)])


def run_l1_moe(inputs, x3_full, counts_all):
    nc = build_l1_moe()
    consts = make_consts()
    shards = _shard_x(x3_full)
    in_maps = []
    for c in range(NCORES):
        cm = np.zeros((8, 1), np.float32)
        cm[:c, 0] = 1.0
        m = {"xin": shards[c],
             "wr": np.ascontiguousarray(inputs["L1_ffn_w_router"]),
             "wg": np.ascontiguousarray(inputs["L1_ffn_w_gate"]),
             "wu": np.ascontiguousarray(inputs["L1_ffn_w_up"]),
             "wd": np.ascontiguousarray(inputs["L1_ffn_w_down"]),
             "call": np.ascontiguousarray(counts_all.astype(np.float32)),
             "cmask": cm}
        m.update(consts)
        in_maps.append(m)
    res = run_bass_kernel_spmd(nc, in_maps, core_ids=list(range(NCORES)))
    return _unshard_x([res.results[c]["xout"] for c in range(NCORES)])



def _shard_x(x):
    xr = np.ascontiguousarray(x.reshape(16, SEQ, DM))
    return [np.ascontiguousarray(xr[2 * c:2 * c + 2].reshape(TOK, DM))
            for c in range(NCORES)]


def _unshard_x(shards):
    out = np.empty((16, SEQ, DM), np.float32)
    for c in range(NCORES):
        out[2 * c:2 * c + 2] = np.asarray(shards[c]).reshape(2, SEQ, DM)
    return out


def run_l0_attn(inputs, x_full):
    nc = build_l0_attn()
    consts = make_consts()
    shards = _shard_x(x_full)
    in_maps = []
    for c in range(NCORES):
        m = {"xin": shards[c],
             "wq": np.ascontiguousarray(inputs["L0_attn_Wq"]),
             "wk": np.ascontiguousarray(inputs["L0_attn_Wk"]),
             "wv": np.ascontiguousarray(inputs["L0_attn_Wv"]),
             "wo": np.ascontiguousarray(inputs["L0_attn_Wo"]),
             "bo": np.ascontiguousarray(inputs["L0_attn_bo"].reshape(1, DM))}
        m.update(consts)
        in_maps.append(m)
    res = run_bass_kernel_spmd(nc, in_maps, core_ids=list(range(NCORES)))
    return _unshard_x([res.results[c]["xout"] for c in range(NCORES)])


def kernel(**inputs):
    x = np.ascontiguousarray(inputs["L0_attn_x"].astype(np.float32))
    x1 = run_l0_attn(inputs, x)
    x2 = run_l0_ffn(inputs, x1)
    x3 = run_l1_attn(inputs, x2)
    counts = run_l1_route(inputs, x3)
    x4 = run_l1_moe(inputs, x3, counts)
    return x4.astype(np.float32)
